# version 60
# baseline (speedup 1.0000x reference)
import numpy as np
from contextlib import ExitStack

import concourse.bass as bass
import concourse.mybir as mybir
from concourse.bass_utils import run_bass_kernel_spmd

F32 = mybir.dt.float32
BF16 = mybir.dt.bfloat16
FP8 = mybir.dt.float8e4
AF = mybir.ActivationFunctionType
ALU = mybir.AluOpType

D = 1024
KD = 8
S = 16
EPS = 1e-5
FP8MAX = 448.0
N0 = 512


def build_nc(n_seq=2, T=2048, C=128, W=16, debug=False, skip_p3_store=False):
    L = T // C
    assert W <= L and T % C == 0
    NSTEP = W + L
    LB = C
    WT = W + T
    NTOK = n_seq * T
    NT = NTOK // 128
    TPS = T // 128
    NB = 2
    NBI = 3

    nc = bass.Bass("TRN2", target_bir_lowering=False, debug=debug)

    x_d = nc.dram_tensor("x", [NTOK, D], F32, kind="ExternalInput").ap()
    wg_d = nc.dram_tensor("w_gate", [D, D], F32, kind="ExternalInput").ap()
    wdel_d = nc.dram_tensor("w_delta", [D, S], F32, kind="ExternalInput").ap()
    win_d = nc.dram_tensor("w_in", [D, S], F32, kind="ExternalInput").ap()
    bg_d = nc.dram_tensor("b_gate", [1, D], F32, kind="ExternalInput").ap()
    bdel_d = nc.dram_tensor("b_delta", [1, S], F32, kind="ExternalInput").ap()
    bin_d = nc.dram_tensor("b_in", [1, S], F32, kind="ExternalInput").ap()
    gam_d = nc.dram_tensor("ln_gamma", [1, D], F32, kind="ExternalInput").ap()
    bet_d = nc.dram_tensor("ln_beta", [1, D], F32, kind="ExternalInput").ap()
    amat_d = nc.dram_tensor("a_mat", [S, S], F32, kind="ExternalInput").ap()
    cmat_d = nc.dram_tensor("c_mat", [S, D], F32, kind="ExternalInput").ap()
    fwT_d = nc.dram_tensor("ffn_wT", [D, D], F32, kind="ExternalInput").ap()
    fb_d = nc.dram_tensor("ffn_b", [1, D], F32, kind="ExternalInput").ap()
    out_d = nc.dram_tensor("out", [NTOK, D], F32, kind="ExternalOutput").ap()
    x2t_d = nc.dram_tensor("x2t_scratch", [NTOK, D], F32).ap()

    st = ExitStack()
    sbuf = lambda n, sh, dt=F32: st.enter_context(nc.sbuf_tensor(n, sh, dt))
    sem = lambda n: st.enter_context(nc.semaphore(n))

    with st:
        wg_hi = sbuf("wg_hi", [128, KD, D], BF16)
        wg_lo = sbuf("wg_lo", [128, KD, D], BF16)
        wdb = sbuf("wdb", [128, KD, 64])
        bdb = sbuf("bdb", [1, 64])
        fb2 = sbuf("fb2", [2, D], BF16)
        ones2b = sbuf("ones2b", [2, 128], BF16)
        cm_hi = sbuf("cm_hi", [S, D], BF16)
        cm_lo = sbuf("cm_lo", [S, D], BF16)
        cm3 = sbuf("cm3", [3 * S, D], BF16)
        ones_r = sbuf("ones_r", [1, 128])
        ones1b = sbuf("ones1b", [1, 128], BF16)
        bgate_bc = sbuf("bgate_bc", [128, D])
        fb_hi = sbuf("fb_hi", [1, D], BF16)
        fb_lo = sbuf("fb_lo", [1, D], BF16)
        wqT = sbuf("wqT", [128, KD, D], FP8)
        a16 = sbuf("a16", [S, S])
        gam_s = sbuf("gam_s", [128, KD])
        bet_s = sbuf("bet_s", [128, KD])
        ident = sbuf("ident", [128, 128])
        ident8 = sbuf("ident8", [128, 128], FP8)
        eps_t = sbuf("eps_t", [128, 1])
        d_tok = sbuf("d_tok", [S, n_seq, WT])
        b_tok = sbuf("b_tok", [S, n_seq, WT])
        h_tok = sbuf("h_tok", [S, n_seq, WT])

        prep_st = ExitStack()
        iog = prep_st.enter_context(nc.sbuf_tensor("iog", [128, 128], mybir.dt.int32))
        iof = prep_st.enter_context(nc.sbuf_tensor("iof", [128, 128], mybir.dt.int32))
        bgf = prep_st.enter_context(nc.sbuf_tensor("bgf", [1, D], F32))
        cmat = prep_st.enter_context(nc.sbuf_tensor("cmat", [S, D], F32))
        fbf = prep_st.enter_context(nc.sbuf_tensor("fbf", [1, D], F32))

        pdq = sem("pdq"); pdh = sem("pdh"); pL = sem("pL"); pcm = sem("pcm")
        sdh = [sem(f"sdh{j}") for j in range(4)]
        pdw = [sem(f"pdw{j}") for j in range(2)]
        pd2a = sem("pd2a"); pd2b = sem("pd2b")
        pg = sem("pg"); pv = sem("pv"); pds = sem("pds"); pfb = sem("pfb")

        with nc.Block() as blk:
            pdn = [0]; pd2n = [0]

            @blk.sync
            def _(sync):
                for c in range(KD):
                    sync.dma_start(gam_s[:, c:c + 1], gam_d[0:1, 128 * c:128 * (c + 1)]).then_inc(pds, 16); pdn[0] += 16
                for c in range(KD):
                    sync.dma_start(bet_s[:, c:c + 1], bet_d[0:1, 128 * c:128 * (c + 1)]).then_inc(pds, 16); pdn[0] += 16
                sync.dma_start(a16[:, :], amat_d[:, :]).then_inc(pds, 16); pdn[0] += 16
                sync.dma_start(cmat[:, :], cmat_d[:, :]).then_inc(pds, 16); pdn[0] += 16

            @blk.scalar
            def _(sc):
                sc.dma_start(fbf[0:1, :], fb_d[0:1, :]).then_inc(pd2a, 16)
                sc.dma_start(bgate_bc[:, :], bass.AP(bg_d.tensor, 0, [[0, 128], [1, D]])).then_inc(pd2a, 16)
                sc.wait_ge(pg, 2)
                for c in range(KD):
                    sc.dma_start(wdb[:, c, 0:S], wdel_d[128 * c:128 * (c + 1), :]).then_inc(pd2b, 16); pd2n[0] += 16
                for c in range(KD):
                    sc.dma_start(wdb[:, c, 32:32 + S], win_d[128 * c:128 * (c + 1), :]).then_inc(pd2b, 16); pd2n[0] += 16
                sc.dma_start(bdb[0:1, 0:S], bdel_d[0:1, :]).then_inc(pd2b, 16); pd2n[0] += 16
                sc.dma_start(bdb[0:1, 32:32 + S], bin_d[0:1, :]).then_inc(pd2b, 16); pd2n[0] += 16
                sc.wait_ge(pv, 2)
                sc.dma_start(fb2[0:1, :], fb_hi[0:1, :]).then_inc(pfb, 16)
                sc.dma_start(fb2[1:2, :], fb_lo[0:1, :]).then_inc(pfb, 16)
                sc.wait_ge(pv, 6)
                sc.dma_start(cm3[0:S, :], cm_hi[:, :]).then_inc(pcm, 16)
                sc.dma_start(cm3[S:2 * S, :], cm_lo[:, :]).then_inc(pcm, 16)
                sc.dma_start(cm3[2 * S:3 * S, :], cm_hi[:, :]).then_inc(pcm, 16)

            @blk.gpsimd
            def _(gp):
                for c in range(KD):
                    gp.dma_start(wqT[:, c, :], fwT_d[128 * c:128 * (c + 1), :]).then_inc(pdq, 16)
                gp.memset(wdb[:, :, :], 0.0).then_inc(pg, 1)
                gp.memset(bdb[0:1, :], 0.0).then_inc(pg, 1)
                gp.memset(ones2b[:, :], 1.0)
                gp.memset(d_tok[:, :, 0:W], 0.0).then_inc(pg, 1)
                gp.memset(b_tok[:, :, 0:W], 0.0).then_inc(pg, 1)
                gp.memset(h_tok[:, :, 0:W], 0.0).then_inc(pg, 1)
                gp.memset(ones_r[0:1, :], 1.0).then_inc(pg, 1)
                gp.memset(ones1b[0:1, :], 1.0).then_inc(pg, 1)
                gp.iota(iog[:, :], [[0, 128]], channel_multiplier=1).then_inc(pg, 1)
                gp.iota(iof[:, :], [[1, 128]], channel_multiplier=0).then_inc(pg, 1)
                gp.memset(eps_t[:, :], EPS).then_inc(pg, 1)

            @blk.vector
            def _(v):
                v.wait_ge(pd2a, 32)
                v.tensor_copy(fb_hi[0:1, :], fbf[0:1, :]).then_inc(pv, 1)
                v.wait_ge(pv, 1)
                v.tensor_tensor(fb_lo[0:1, :], fbf[0:1, :], fb_hi[0:1, :], ALU.subtract).then_inc(pv, 1)
                v.wait_ge(pg, 9)
                v.tensor_tensor(ident[:, :], iog[:, :], iof[:, :], ALU.is_equal).then_inc(pv, 1)
                v.wait_ge(pv, 3)
                v.tensor_copy(ident8[:, :], ident[:, :]).then_inc(pv, 1)
                v.wait_ge(pds, pdn[0])
                v.tensor_copy(cm_hi[:, :], cmat[:, :]).then_inc(pv, 1)
                v.wait_ge(pv, 5)
                v.tensor_tensor(cm_lo[:, :], cmat[:, :], cm_hi[:, :], ALU.subtract).then_inc(pv, 1)
                v.wait_ge(pfb, 32)
                v.wait_ge(pd2b, pd2n[0])

        prep_st.close()

        NBX = 4
        NBP = 4
        xns = sbuf("xns", [128, NBP, D])
        xn2s = sbuf("xn2s", [128, NBI, D])
        x2T = sbuf("x2T", [128, 3, KD, 128])
        part = sbuf("part", [128, NB, D])
        s_tmp = sbuf("s_tmp", [128, 3, D])
        bn6a = sbuf("bn6a", [128, 12])
        nm2t = sbuf("nm2t", [128, 2])
        junk_v = sbuf("junk_v", [128, 16])
        junk_s = sbuf("junk_s", [128, 16])
        junk_g = sbuf("junk_g", [128, 16])
        ve1 = sbuf("ve1", [128, 1])
        ve2 = sbuf("ve2", [128, 1])
        rv1 = sbuf("rv1", [128, NT])
        rv2 = sbuf("rv2", [128, NT])
        mv1a = sbuf("mv1a", [128, NT, 2])
        mv2a = sbuf("mv2a", [128, NT, 2])
        r1a = sbuf("r1a", [128, NT])
        r2a = sbuf("r2a", [128, NT])
        hdb = sbuf("hdb", [S, 2, 2, LB])

        sx = [sem(f"sx{j}") for j in range(NBX)]
        spo = [sem(f"spo{j}") for j in range(NB)]
        v1 = sem("v1"); s1 = sem("s1"); t1 = sem("t1"); g1 = sem("g1")
        v2s = sem("v2s"); s2s = sem("s2s"); t2s = sem("t2s")
        spx = [sem(f"spx{j}") for j in range(3)]
        sxt = [sem(f"sxt{j}") for j in range(3)]
        spi = [sem(f"spi{j}") for j in range(NBP)]
        so3 = [sem(f"so3_{j}") for j in range(NB)]
        v3 = sem("v3"); s3 = sem("s3"); t3 = sem("t3"); g3 = sem("g3")

        jvn = [0]; jsn = [0]; jgn = [0]

        def _fill(eng, semh, n, junk, ctr):
            if junk is None:
                eng.ldweights(ones1b[0:1, 0:1]).then_inc(semh, n)
                return
            c = ctr[0] % 16; ctr[0] += 1
            if junk is junk_s:
                eng.copy(junk[:, c:c + 1], eps_t[:, :]).then_inc(semh, n)
            else:
                eng.tensor_copy(junk[:, c:c + 1], eps_t[:, :]).then_inc(semh, n)


        nV, nS, nT, nG = 15, 13, 35, 2
        WG0 = max(0, NT - 12)
        p1_st = ExitStack()
        xs = p1_st.enter_context(nc.sbuf_tensor("xs", [128, NBX, D], F32))
        wgf_r = p1_st.enter_context(nc.sbuf_tensor("wgf_r", [128, 2, D], F32))
        xq = p1_st.enter_context(nc.sbuf_tensor("xq", [128, NB, D], FP8))
        xqT = p1_st.enter_context(nc.sbuf_tensor("xqT", [128, NB, KD, 128], FP8))
        ps_sh = [p1_st.enter_context(nc.psum_tensor(f"ps_sh{i}", [128, D], F32)) for i in range(NB)]
        ps_db = p1_st.enter_context(nc.psum_tensor("ps_db", [64, NB, 128], F32))
        ps_q8 = p1_st.enter_context(nc.psum_tensor("ps_q8", [128, 2 * D], FP8))
        ps_ffn = p1_st.enter_context(nc.psum_tensor("ps_ffn", [128, D], F32))

        with nc.Block() as blk:
            @blk.sync
            def _(sync):
                for i in range(NT + 6):
                    j = i - 6
                    if 0 <= j < NT:
                        sync.wait_ge(v1, (j + 4) * nV + 13)
                        sync.dma_start(out_d[128 * j:128 * (j + 1), :], part[:, j % NB, :]).then_inc(spo[j % NB], 16)
                    j2 = i - 5
                    if 0 <= j2 < NT:
                        sync.wait_ge(s1, (i - 2) * nS + 12)
                        sync.dma_start(x2t_d[128 * j2:128 * (j2 + 1), :],
                                       x2T[:, j2 % 3, :, :]).then_inc(spx[j2 % 3], 16)
                    if i < NT:
                        if i >= NBX:
                            sync.wait_ge(g1, (i - 2) * nG + 2)
                            sync.wait_ge(v1, (i - 3) * nV + 7)
                        sync.dma_start(xs[:, i % NBX, :], x_d[128 * i:128 * (i + 1), :]).then_inc(sx[i % NBX], 16)
                for j in range(NB):
                    sync.wait_ge(spo[j], 16 * ((NT - j + NB - 1) // NB))

            @blk.vector
            def _(v):
                for i in range(NT + 4):
                    j = i - 2
                    if 0 <= j < NT:
                        v.wait_ge(s1, i * nS + 1)
                        v.tensor_scalar(nm2t[:, j % 2:j % 2 + 1], mv2a[:, j, 0:1], r2a[:, j:j + 1],
                                        -1.0, ALU.mult, ALU.mult).then_inc(v1, 1)
                    else:
                        _fill(v, v1, 1, junk_v, jvn)
                    if i < NT:
                        sl = i % NBX
                        v.wait_ge(sx[sl], 16 * (i // NBX + 1))
                        v.bn_stats(bn6a[:, 0:6], xs[:, sl, 0:N0]).then_inc(v1, 1)
                        v.bn_stats(bn6a[:, 6:12], xs[:, sl, N0:D]).then_inc(v1, 1)
                        v.wait_ge(v1, i * nV + 3)
                        v.bn_aggr(mv1a[:, i, :], bn6a[:, :]).then_inc(v1, 1)
                        v.wait_ge(v1, i * nV + 4)
                        v.tensor_scalar(ve1[:, :], mv1a[:, i, 1:2], EPS, None, ALU.add).then_inc(v1, 1)
                        v.wait_ge(v1, i * nV + 5)
                        v.reciprocal(rv1[:, i:i + 1], ve1[:, :]).then_inc(v1, 1)
                    else:
                        _fill(v, v1, 5, junk_v, jvn)
                    j = i - 1
                    if 0 <= j < NT:
                        sl = j % NBX
                        v.wait_ge(s1, i * nS + 2)
                        if j >= NBI:
                            v.wait_ge(g1, (i - 2) * nG + 2)
                            v.wait_ge(s1, (i - 2) * nS + 3)
                        v.tensor_scalar(xns[:, j % NBI, :], xs[:, sl, :], mv1a[:, j, 0:1], r1a[:, j:j + 1],
                                        ALU.subtract, ALU.mult).then_inc(v1, 1)
                        v.wait_ge(v1, i * nV + 7)
                        v.bn_stats(bn6a[:, 0:6], xns[:, j % NBI, 0:N0]).then_inc(v1, 1)
                        v.bn_stats(bn6a[:, 6:12], xns[:, j % NBI, N0:D]).then_inc(v1, 1)
                        v.wait_ge(v1, i * nV + 9)
                        v.bn_aggr(mv2a[:, j, :], bn6a[:, :]).then_inc(v1, 1)
                        v.wait_ge(v1, i * nV + 10)
                        v.tensor_scalar(ve2[:, :], mv2a[:, j, 1:2], EPS, None, ALU.add).then_inc(v1, 1)
                        v.wait_ge(v1, i * nV + 11)
                        v.reciprocal(rv2[:, j:j + 1], ve2[:, :]).then_inc(v1, 1)
                    else:
                        _fill(v, v1, 6, junk_v, jvn)
                    j = i - 4
                    if 0 <= j < NT:
                        v.wait_ge(t1, i * nT + 35)
                        v.wait_ge(g1, (i - 2) * nG + 2)
                        if j >= NB:
                            v.wait_ge(spo[j % NB], 16 * (j // NB))
                        v.tensor_add(part[:, j % NB, :], s_tmp[:, j % 3, :],
                                     ps_ffn[:, :]).then_inc(v1, 1)
                    else:
                        _fill(v, v1, 1, junk_v, jvn)
                    j = i - 4
                    if 0 <= j < NT:
                        sq = j // TPS
                        tr = (j % TPS) * 128
                        v.wait_ge(t1, i * nT + 25)
                        v.tensor_copy(d_tok[:, sq, W + tr:W + tr + 128],
                                      ps_db[0:16, j % NB, :]).then_inc(v1, 1)
                        v.tensor_copy(b_tok[:, sq, W + tr:W + tr + 128],
                                      ps_db[32:48, j % NB, :]).then_inc(v1, 1)
                    else:
                        _fill(v, v1, 2, junk_v, jvn)

            @blk.scalar
            def _(sc):
                for i in range(NT + 4):
                    j = i - 2
                    if 0 <= j < NT:
                        sc.wait_ge(v1, (i - 1) * nV + 12)
                        sc.activation(r2a[:, j:j + 1], rv2[:, j:j + 1], AF.Sqrt).then_inc(s1, 1)
                    else:
                        _fill(sc, s1, 1, junk_s, jsn)
                    j = i - 1
                    if 0 <= j < NT:
                        sc.wait_ge(v1, (i - 1) * nV + 6)
                        sc.activation(r1a[:, j:j + 1], rv1[:, j:j + 1], AF.Sqrt).then_inc(s1, 1)
                    else:
                        _fill(sc, s1, 1, junk_s, jsn)
                    j = i - 2
                    if 0 <= j < NT:
                        sc.wait_ge(v1, (i - 1) * nV + 7)
                        if j >= NB:
                            sc.wait_ge(t1, (i - 1) * nT + 16)
                        sc.copy(xq[:, j % NB, 0:N0], xns[:, j % NBI, 0:N0]).then_inc(s1, 1)
                    else:
                        _fill(sc, s1, 1, junk_s, jsn)
                    j = i - 2
                    if 0 <= j < NT:
                        sc.wait_ge(v1, i * nV + 1)
                        if j >= NB:
                            sc.wait_ge(t1, (i - 2) * nT + 8)
                        sc.activation(xn2s[:, j % NBI, :], xns[:, j % NBI, :], AF.Identity,
                                      bias=nm2t[:, j % 2:j % 2 + 1],
                                      scale=r2a[:, j:j + 1]).then_inc(s1, 1)
                    else:
                        _fill(sc, s1, 1, junk_s, jsn)
                    j = i - 3
                    if 0 <= j < NT:
                        sc.wait_ge(t1, i * nT + 8)
                        if j >= 3:
                            sc.wait_ge(spx[j % 3], 16 * (j // 3))
                        for c in range(KD):
                            sc.activation(x2T[:, j % 3, c, :], ps_sh[j % NB][:, 128 * c:128 * (c + 1)],
                                          AF.Identity, bias=bet_s[:, c:c + 1],
                                          scale=gam_s[:, c:c + 1]).then_inc(s1, 1)
                        sc.wait_ge(t1, i * nT + 16)
                        sc.copy(xqT[:, j % NB, :, :].rearrange("p a b -> p (a b)"),
                                bass.AP(ps_q8, 0, [[2 * D, 128], [2, D]])).then_inc(s1, 1)
                    else:
                        _fill(sc, s1, 9, junk_s, jsn)

            @blk.gpsimd
            def _(gp):
                for i in range(NT + 4):
                    j = i - 2
                    if 0 <= j < NT:
                        gp.wait_ge(v1, (i - 1) * nV + 7)
                        if j >= NB:
                            gp.wait_ge(t1, (i - 1) * nT + 16)
                        gp.tensor_scalar(xq[:, j % NB, N0:D], xns[:, j % NBI, N0:D], FP8MAX, -FP8MAX,
                                         ALU.min, ALU.max).then_inc(g1, 1)
                    else:
                        _fill(gp, g1, 1, junk_g, jgn)
                    j = i - 2
                    if 0 <= j < NT:
                        if j >= 3:
                            gp.wait_ge(v1, (i - 1) * nV + 13)
                        gp.tensor_add(s_tmp[:, j % 3, :], xs[:, j % NBX, :],
                                      xns[:, j % NBI, :]).then_inc(g1, 1)
                    else:
                        _fill(gp, g1, 1, junk_g, jgn)
                    if i == 0:
                        gp.dma_start(wg_hi[:, :, :],
                                     bass.AP(wg_d.tensor, 0, [[D, 128], [128 * D, KD], [1, D]])).then_inc(pdh, 16)
                        for c in range(2):
                            gp.dma_start(wgf_r[:, c, :],
                                         wg_d[128 * c:128 * (c + 1), :]).then_inc(pdw[c], 16)
                    c = i - WG0
                    if 0 <= c < KD:
                        if c == 0:
                            gp.wait_ge(pdh, 16)
                        gp.wait_ge(pdw[c % 2], 16 * (c // 2 + 1))
                        gp.tensor_tensor(wg_lo[:, c, :], wgf_r[:, c % 2, :],
                                         wg_hi[:, c, :], ALU.subtract).then_inc(pL, 1)
                        if c + 2 < KD:
                            gp.wait_ge(pL, c + 1)
                            gp.dma_start(wgf_r[:, c % 2, :],
                                         wg_d[128 * (c + 2):128 * (c + 3), :]).then_inc(pdw[c % 2], 16)

            @blk.tensor
            def _(t):
                for i in range(NT + 4):
                    j = i - 3
                    if 0 <= j < NT:
                        t.wait_ge(s1, (j + 2) * nS + 4)
                        if j >= NB:
                            t.wait_ge(s1, (i - 2) * nS + 12)
                        for c in range(KD):
                            t.transpose(ps_sh[j % NB][:, 128 * c:128 * (c + 1)],
                                        xn2s[:, j % NBI, 128 * c:128 * (c + 1)],
                                        ident[:, :]).then_inc(t1, 1)
                        t.wait_ge(s1, (i - 1) * nS + 3)
                        if j >= 1:
                            t.wait_ge(s1, (i - 1) * nS + 13)
                        for c in range(KD // 2):
                            t.transpose(bass.AP(ps_q8, 2 * 128 * c, [[2 * D, 128], [2, 128]]),
                                        xq[:, j % NB, 128 * c:128 * (c + 1)],
                                        ident8[:, :]).then_inc(t1, 1)
                        t.wait_ge(g1, (i - 1) * nG + 1)
                        for c in range(KD // 2, KD):
                            t.transpose(bass.AP(ps_q8, 2 * 128 * c, [[2 * D, 128], [2, 128]]),
                                        xq[:, j % NB, 128 * c:128 * (c + 1)],
                                        ident8[:, :]).then_inc(t1, 1)
                    else:
                        _fill(t, t1, 16, None, None)
                    j = i - 4
                    if 0 <= j < NT:
                        t.wait_ge(s1, (i - 1) * nS + 12)
                        if j >= 1:
                            t.wait_ge(v1, (i - 1) * nV + 15)
                        for c in range(KD):
                            t.matmul(ps_db[:, j % NB, :], wdb[:, c, :], x2T[:, j % 3, c, :],
                                     start=(c == 0), stop=False).then_inc(t1, 1)
                        t.matmul(ps_db[:, j % NB, :], bdb[0:1, :], ones_r[0:1, :],
                                 start=False, stop=True).then_inc(t1, 1)
                        t.wait_ge(s1, (i - 1) * nS + 13)
                        if j == 0:
                            t.wait_ge(pdq, 16 * KD)
                        if j >= 1:
                            t.wait_ge(v1, (i - 1) * nV + 13)
                        for h in range(2):
                            for c in range(0, KD, 2):
                                t.matmul(ps_ffn[:, h * N0:(h + 1) * N0], xqT[:, j % NB, c:c + 2, :],
                                         wqT[:, c:c + 2, h * N0:(h + 1) * N0],
                                         start=(c == 0), stop=False,
                                         perf_mode=mybir.MatmulPerfMode.DoubleRow).then_inc(t1, 1)
                            t.matmul(ps_ffn[:, h * N0:(h + 1) * N0], ones2b[0:2, :],
                                     fb2[0:2, h * N0:(h + 1) * N0], start=False, stop=True).then_inc(t1, 1)
                    else:
                        _fill(t, t1, 19, None, None)

        p1_st.close()

        p2_st = ExitStack()
        ps_sc = [p2_st.enter_context(nc.psum_tensor(f"ps_sc{i}", [S, LB], F32)) for i in range(4)]

        def ap_tok(tens, g, col, n, step):
            return bass.AP(tens, g * WT + col, [[n_seq * WT, S], [step, n]])

        with nc.Block() as blk:
            @blk.scalar
            def _(sc):
                sc.wait_ge(v1, (NT + 4) * nV)
                sc.activation(d_tok[:, :, :], d_tok[:, :, :], AF.Sigmoid).then_inc(s2s, 1)
                for i in range(NSTEP):
                    for g in range(n_seq):
                        if i == 0:
                            sc.wait_ge(t2s, 1 + g)
                        else:
                            sc.wait_ge(t2s, 2 + 4 * (i - 1) + 2 + 2 * g)
                        sc.activation(ap_tok(h_tok, g, i, LB, L),
                                      ps_sc[2 * g + i % 2][:, :], AF.Tanh).then_inc(s2s, 1)

            @blk.vector
            def _(v):
                for i in range(1, NSTEP):
                    for g in range(n_seq):
                        v.wait_ge(s2s, 1 + 2 * (i - 1) + g + 1)
                        v.tensor_mul(hdb[:, i % 2, g, :],
                                     ap_tok(h_tok, g, i - 1, LB, L),
                                     ap_tok(d_tok, g, i, LB, L)).then_inc(v2s, 1)

            @blk.tensor
            def _(t):
                t.wait_ge(s2s, 1)
                t.wait_ge(v1, (NT + 4) * nV)
                t.matmul(ps_sc[0][:, :], ident[0:S, 0:S], ap_tok(b_tok, 0, 0, LB, L),
                         start=True, stop=True).then_inc(t2s, 1)
                t.matmul(ps_sc[2][:, :], ident[0:S, 0:S], ap_tok(b_tok, 1, 0, LB, L),
                         start=True, stop=True).then_inc(t2s, 1)
                for i in range(1, NSTEP):
                    for g in range(n_seq):
                        if i >= 2:
                            t.wait_ge(s2s, 1 + 2 * (i - 2) + g + 1)
                        t.matmul(ps_sc[2 * g + i % 2][:, :], ident[0:S, 0:S],
                                 ap_tok(b_tok, g, i, LB, L),
                                 start=True, stop=False).then_inc(t2s, 1)
                        t.wait_ge(v2s, 2 * (i - 1) + g + 1)
                        t.matmul(ps_sc[2 * g + i % 2][:, :], a16[:, :], hdb[:, i % 2, g, :],
                                 start=False, stop=True).then_inc(t2s, 1)

        p2_st.close()

        nV3, nS3, nT3, nG3 = 8, 2, 54, 1
        p3_st = ExitStack()
        t3hi = p3_st.enter_context(nc.sbuf_tensor("t3hi", [128, 3, KD, 128], BF16))
        t3lo = p3_st.enter_context(nc.sbuf_tensor("t3lo", [128, 3, KD, 128], BF16))
        gsb = p3_st.enter_context(nc.sbuf_tensor("gsb", [128, NB, D], F32))
        hs3 = p3_st.enter_context(nc.sbuf_tensor("hs3", [3 * S, NBP, 128], BF16))
        hs_tl = p3_st.enter_context(nc.sbuf_tensor("hs_tl", [S, NBP, 128], BF16))
        ps_sh3 = [p3_st.enter_context(nc.psum_tensor(f"ps_sh3{i}", [128, D], F32)) for i in range(3)]
        ps_g = [p3_st.enter_context(nc.psum_tensor(f"ps_g{i}", [128, N0], F32)) for i in range(2)]

        with nc.Block() as blk:
            @blk.sync
            def _(sync):
                for k in range(NT + 5):
                    j = k - 5
                    if 0 <= j < NT:
                        sync.wait_ge(g3, (j + 4) * nG3)
                        sync.dma_start(out_d[128 * j:128 * (j + 1), :], part[:, j % NB, :]).then_inc(so3[j % NB], 16)
                    if k < NT:
                        if k < 3:
                            sync.wait_ge(spx[k], 16 * ((NT - k + 2) // 3))
                        else:
                            sync.wait_ge(v3, (k - 1) * nV3 + 2)
                        sync.dma_start(x2T[:, k % 3, :, :],
                                       x2t_d[128 * k:128 * (k + 1), :]).then_inc(sxt[k % 3], 16)
                        if k >= NBP:
                            sync.wait_ge(g3, k * nG3)
                        sync.dma_start(xns[:, k % NBP, :], out_d[128 * k:128 * (k + 1), :]).then_inc(spi[k % NBP], 16)
                        sync.wait_ge(v3, k * nV3 + 8)
                        sync.dma_start(hs3[S:2 * S, k % NBP, :], hs3[0:S, k % NBP, :]).then_inc(sdh[k % NBP], 16)
                        sync.dma_start(hs3[2 * S:3 * S, k % NBP, :], hs_tl[:, k % NBP, :]).then_inc(sdh[k % NBP], 16)
                for j in range(NB):
                    sync.wait_ge(so3[j], 16 * ((NT - j + NB - 1) // NB))

            @blk.vector
            def _(v):
                for k in range(NT + 3):
                    j = k - 2
                    if 0 <= j < NT:
                        v.wait_ge(sxt[j % 3], 16 * (j // 3 + 1))
                        if j >= 3:
                            v.wait_ge(t3, (k - 2) * nT3 + 40)
                        v.tensor_copy(t3hi[:, j % 3, :, :], x2T[:, j % 3, :, :]).then_inc(v3, 1)
                        v.wait_ge(v3, k * nV3 + 1)
                        if j >= 3:
                            v.wait_ge(t3, (k - 2) * nT3 + 48)
                        v.tensor_tensor(t3lo[:, j % 3, :, :], x2T[:, j % 3, :, :], t3hi[:, j % 3, :, :],
                                        ALU.subtract).then_inc(v3, 1)
                    else:
                        _fill(v, v3, 2, junk_v, jvn)
                    j = k - 3
                    if 0 <= j < NT:
                        v.wait_ge(t3, k * nT3 + 24)
                        if j >= NB:
                            v.wait_ge(s3, (k - 2) * nS3 + 1)
                        v.tensor_add(gsb[:, j % NB, 0:N0], ps_g[0][:, :], bgate_bc[:, 0:N0]).then_inc(v3, 1)
                        v.wait_ge(t3, k * nT3 + 48)
                        if j >= NB:
                            v.wait_ge(s3, (k - 2) * nS3 + 2)
                        v.tensor_add(gsb[:, j % NB, N0:D], ps_g[1][:, :], bgate_bc[:, N0:D]).then_inc(v3, 1)
                        v.wait_ge(t3, k * nT3 + 54)
                        v.wait_ge(s3, k * nS3 + 1)
                        if k >= 4:
                            v.wait_ge(g3, k * nG3)
                        v.tensor_mul(s_tmp[:, j % NB, 0:N0], ps_sh3[j % 3][:, 0:N0],
                                     gsb[:, j % NB, 0:N0]).then_inc(v3, 1)
                        v.wait_ge(s3, k * nS3 + 2)
                        v.tensor_mul(s_tmp[:, j % NB, N0:D], ps_sh3[j % 3][:, N0:D],
                                     gsb[:, j % NB, N0:D]).then_inc(v3, 1)
                    else:
                        _fill(v, v3, 4, junk_v, jvn)
                    if k < NT:
                        sq3 = k // TPS
                        tr3 = (k % TPS) * 128
                        if k == 0:
                            v.wait_ge(s2s, 1 + 2 * NSTEP)
                        if k >= NBP:
                            v.wait_ge(t3, (k - 1) * nT3 + 54)
                        v.tensor_copy(hs3[0:S, k % NBP, :],
                                      h_tok[:, sq3, W + tr3:W + tr3 + 128]).then_inc(v3, 1)
                        v.wait_ge(v3, k * nV3 + 7)
                        v.tensor_tensor(hs_tl[:, k % NBP, :],
                                        h_tok[:, sq3, W + tr3:W + tr3 + 128],
                                        hs3[0:S, k % NBP, :], ALU.subtract).then_inc(v3, 1)
                    else:
                        _fill(v, v3, 2, junk_v, jvn)

            @blk.scalar
            def _(sc):
                for k in range(NT + 3):
                    j = k - 3
                    if 0 <= j < NT:
                        sc.wait_ge(v3, k * nV3 + 3)
                        sc.activation(gsb[:, j % NB, 0:N0], gsb[:, j % NB, 0:N0], AF.Sigmoid).then_inc(s3, 1)
                        sc.wait_ge(v3, k * nV3 + 4)
                        sc.activation(gsb[:, j % NB, N0:D], gsb[:, j % NB, N0:D], AF.Sigmoid).then_inc(s3, 1)
                    else:
                        _fill(sc, s3, 2, junk_s, jsn)

            @blk.gpsimd
            def _(gp):
                for k in range(NT + 3):
                    j = k - 3
                    if 0 <= j < NT:
                        gp.wait_ge(v3, k * nV3 + 6)
                        gp.wait_ge(spi[j % NBP], 16 * (j // NBP + 1))
                        if j >= NB:
                            gp.wait_ge(so3[j % NB], 16 * ((j - NB) // NB + 1))
                        gp.tensor_add(part[:, j % NB, :], s_tmp[:, j % NB, :],
                                      xns[:, j % NBP, :]).then_inc(g3, 1)
                    else:
                        _fill(gp, g3, 1, junk_g, jgn)

            @blk.tensor
            def _(t):
                for k in range(NT + 3):
                    j = k - 3
                    if 0 <= j < NT:
                        t.wait_ge(v3, (k - 1) * nV3 + 2)
                        if j == 0:
                            t.wait_ge(pL, KD)
                        for h in range(2):
                            if j >= 1:
                                t.wait_ge(v3, (k - 1) * nV3 + 3 + h)
                            for c in range(KD):
                                t.matmul(ps_g[h][:, :], t3hi[:, j % 3, c, :],
                                         wg_hi[:, c, h * N0:(h + 1) * N0],
                                         start=(c == 0), stop=False).then_inc(t3, 1)
                                t.matmul(ps_g[h][:, :], t3hi[:, j % 3, c, :],
                                         wg_lo[:, c, h * N0:(h + 1) * N0],
                                         start=False, stop=False).then_inc(t3, 1)
                            for c in range(KD):
                                t.matmul(ps_g[h][:, :], t3lo[:, j % 3, c, :],
                                         wg_hi[:, c, h * N0:(h + 1) * N0],
                                         start=False, stop=(c == KD - 1)).then_inc(t3, 1)
                        t.wait_ge(v3, (k - 3) * nV3 + 8)
                        t.wait_ge(sdh[j % NBP], 32 * (j // NBP + 1))
                        if j == 0:
                            t.wait_ge(pcm, 48)
                        for h in range(2):
                            t.matmul(ps_sh3[j % 3][:, h * N0:(h + 1) * N0], hs3[:, j % NBP, :],
                                     cm3[:, h * N0:(h + 1) * N0], start=True, stop=True).then_inc(t3, 1)
                            t.ldweights(ones1b[0:1, 0:1]).then_inc(t3, 1)
                            t.ldweights(ones1b[0:1, 0:1]).then_inc(t3, 1)
                    else:
                        _fill(t, t3, 54, None, None)
        p3_st.close()

    return nc


def kernel(L0_attn_x, L0_attn_ln_gamma, L0_attn_ln_beta, L0_attn_A, L0_attn_C,
           L0_attn_w_delta, L0_attn_b_delta, L0_attn_w_in, L0_attn_b_in,
           L0_attn_w_gate, L0_attn_b_gate, L0_ffn_weight, L0_ffn_bias):
    B, T, _ = L0_attn_x.shape
    n_cores = 8
    spc = B // n_cores
    x = np.ascontiguousarray(np.asarray(L0_attn_x, dtype=np.float32))
    common = {
        "w_gate": np.ascontiguousarray(np.asarray(L0_attn_w_gate, np.float32)),
        "w_delta": np.ascontiguousarray(np.asarray(L0_attn_w_delta, np.float32)),
        "w_in": np.ascontiguousarray(np.asarray(L0_attn_w_in, np.float32)),
        "b_gate": np.asarray(L0_attn_b_gate, np.float32).reshape(1, -1),
        "b_delta": np.asarray(L0_attn_b_delta, np.float32).reshape(1, -1),
        "b_in": np.asarray(L0_attn_b_in, np.float32).reshape(1, -1),
        "ln_gamma": np.asarray(L0_attn_ln_gamma, np.float32).reshape(1, -1),
        "ln_beta": np.asarray(L0_attn_ln_beta, np.float32).reshape(1, -1),
        "a_mat": np.ascontiguousarray(np.asarray(L0_attn_A, np.float32)),
        "c_mat": np.ascontiguousarray(np.asarray(L0_attn_C, np.float32)),
        "ffn_wT": np.ascontiguousarray(np.asarray(L0_ffn_weight, np.float32).T),
        "ffn_b": np.asarray(L0_ffn_bias, np.float32).reshape(1, -1),
    }
    in_maps = []
    for c in range(n_cores):
        m = dict(common)
        m["x"] = np.ascontiguousarray(x[c * spc:(c + 1) * spc].reshape(spc * T, -1))
        in_maps.append(m)
    nc = build_nc(n_seq=spc, T=T)
    res = run_bass_kernel_spmd(nc, in_maps, core_ids=list(range(n_cores)))
    outs = [np.asarray(r["out"]).reshape(spc, T, -1) for r in res.results]
    return np.concatenate(outs, axis=0)


# version 61
# speedup vs baseline: 1.0241x; 1.0241x over previous
import numpy as np
from contextlib import ExitStack

import concourse.bass as bass
import concourse.mybir as mybir
from concourse.bass_utils import run_bass_kernel_spmd

F32 = mybir.dt.float32
BF16 = mybir.dt.bfloat16
FP8 = mybir.dt.float8e4
AF = mybir.ActivationFunctionType
ALU = mybir.AluOpType

D = 1024
KD = 8
S = 16
EPS = 1e-5
FP8MAX = 448.0
N0 = 512


def build_nc(n_seq=2, T=2048, C=128, W=16, debug=False, skip_p3_store=False):
    L = T // C
    assert W <= L and T % C == 0
    NSTEP = W + L
    LB = C
    WT = W + T
    NTOK = n_seq * T
    NT = NTOK // 128
    TPS = T // 128
    NB = 2
    NBI = 3

    nc = bass.Bass("TRN2", target_bir_lowering=False, debug=debug)

    x_d = nc.dram_tensor("x", [NTOK, D], F32, kind="ExternalInput").ap()
    wg_d = nc.dram_tensor("w_gate", [D, D], F32, kind="ExternalInput").ap()
    wdel_d = nc.dram_tensor("w_delta", [D, S], F32, kind="ExternalInput").ap()
    win_d = nc.dram_tensor("w_in", [D, S], F32, kind="ExternalInput").ap()
    bg_d = nc.dram_tensor("b_gate", [1, D], F32, kind="ExternalInput").ap()
    bdel_d = nc.dram_tensor("b_delta", [1, S], F32, kind="ExternalInput").ap()
    bin_d = nc.dram_tensor("b_in", [1, S], F32, kind="ExternalInput").ap()
    gam_d = nc.dram_tensor("ln_gamma", [1, D], F32, kind="ExternalInput").ap()
    bet_d = nc.dram_tensor("ln_beta", [1, D], F32, kind="ExternalInput").ap()
    amat_d = nc.dram_tensor("a_mat", [S, S], F32, kind="ExternalInput").ap()
    cmat_d = nc.dram_tensor("c_mat", [S, D], F32, kind="ExternalInput").ap()
    fwT_d = nc.dram_tensor("ffn_wT", [D, D], F32, kind="ExternalInput").ap()
    fb_d = nc.dram_tensor("ffn_b", [1, D], F32, kind="ExternalInput").ap()
    out_d = nc.dram_tensor("out", [NTOK, D], F32, kind="ExternalOutput").ap()
    x2t_d = nc.dram_tensor("x2t_scratch", [NTOK, D], F32).ap()

    st = ExitStack()
    sbuf = lambda n, sh, dt=F32: st.enter_context(nc.sbuf_tensor(n, sh, dt))
    sem = lambda n: st.enter_context(nc.semaphore(n))

    with st:
        wg_hi = sbuf("wg_hi", [128, KD, D], BF16)
        wg_lo = sbuf("wg_lo", [128, KD, D], BF16)
        wdb = sbuf("wdb", [128, KD, 64])
        bdb = sbuf("bdb", [1, 64])
        fb2 = sbuf("fb2", [2, D], BF16)
        ones2b = sbuf("ones2b", [2, 128], BF16)
        cm_hi = sbuf("cm_hi", [S, D], BF16)
        cm_lo = sbuf("cm_lo", [S, D], BF16)
        cm3 = sbuf("cm3", [3 * S, D], BF16)
        ones_r = sbuf("ones_r", [1, 128])
        ones1b = sbuf("ones1b", [1, 128], BF16)
        bgate_bc = sbuf("bgate_bc", [128, D])
        fb_hi = sbuf("fb_hi", [1, D], BF16)
        fb_lo = sbuf("fb_lo", [1, D], BF16)
        wqT = sbuf("wqT", [128, KD, D], FP8)
        a16 = sbuf("a16", [S, S])
        gam_s = sbuf("gam_s", [128, KD])
        bet_s = sbuf("bet_s", [128, KD])
        ident = sbuf("ident", [128, 128])
        ident8 = sbuf("ident8", [128, 128], FP8)
        eps_t = sbuf("eps_t", [128, 1])
        d_tok = sbuf("d_tok", [S, n_seq, WT])
        b_tok = sbuf("b_tok", [S, n_seq, WT])
        h_tok = sbuf("h_tok", [S, n_seq, WT])

        prep_st = ExitStack()
        iog = prep_st.enter_context(nc.sbuf_tensor("iog", [128, 128], mybir.dt.int32))
        iof = prep_st.enter_context(nc.sbuf_tensor("iof", [128, 128], mybir.dt.int32))
        bgf = prep_st.enter_context(nc.sbuf_tensor("bgf", [1, D], F32))
        cmat = prep_st.enter_context(nc.sbuf_tensor("cmat", [S, D], F32))
        fbf = prep_st.enter_context(nc.sbuf_tensor("fbf", [1, D], F32))

        pdq = sem("pdq"); pdh = sem("pdh"); pL = sem("pL"); pcm = sem("pcm")
        sdh = [sem(f"sdh{j}") for j in range(4)]
        pdw = [sem(f"pdw{j}") for j in range(2)]
        pd2a = sem("pd2a"); pd2b = sem("pd2b")
        pg = sem("pg"); pv = sem("pv"); pds = sem("pds"); pfb = sem("pfb")

        with nc.Block() as blk:
            pdn = [0]; pd2n = [0]

            @blk.sync
            def _(sync):
                for c in range(KD):
                    sync.dma_start(gam_s[:, c:c + 1], gam_d[0:1, 128 * c:128 * (c + 1)]).then_inc(pds, 16); pdn[0] += 16
                for c in range(KD):
                    sync.dma_start(bet_s[:, c:c + 1], bet_d[0:1, 128 * c:128 * (c + 1)]).then_inc(pds, 16); pdn[0] += 16
                sync.dma_start(a16[:, :], amat_d[:, :]).then_inc(pds, 16); pdn[0] += 16
                sync.dma_start(cmat[:, :], cmat_d[:, :]).then_inc(pds, 16); pdn[0] += 16

            @blk.scalar
            def _(sc):
                sc.dma_start(fbf[0:1, :], fb_d[0:1, :]).then_inc(pd2a, 16)
                sc.dma_start(bgate_bc[:, :], bass.AP(bg_d.tensor, 0, [[0, 128], [1, D]])).then_inc(pd2a, 16)
                sc.wait_ge(pg, 2)
                for c in range(KD):
                    sc.dma_start(wdb[:, c, 0:S], wdel_d[128 * c:128 * (c + 1), :]).then_inc(pd2b, 16); pd2n[0] += 16
                for c in range(KD):
                    sc.dma_start(wdb[:, c, 32:32 + S], win_d[128 * c:128 * (c + 1), :]).then_inc(pd2b, 16); pd2n[0] += 16
                sc.dma_start(bdb[0:1, 0:S], bdel_d[0:1, :]).then_inc(pd2b, 16); pd2n[0] += 16
                sc.dma_start(bdb[0:1, 32:32 + S], bin_d[0:1, :]).then_inc(pd2b, 16); pd2n[0] += 16
                sc.wait_ge(pv, 2)
                sc.dma_start(fb2[0:1, :], fb_hi[0:1, :]).then_inc(pfb, 16)
                sc.dma_start(fb2[1:2, :], fb_lo[0:1, :]).then_inc(pfb, 16)
                sc.wait_ge(pv, 6)
                sc.dma_start(cm3[0:S, :], cm_hi[:, :]).then_inc(pcm, 16)
                sc.dma_start(cm3[S:2 * S, :], cm_lo[:, :]).then_inc(pcm, 16)
                sc.dma_start(cm3[2 * S:3 * S, :], cm_hi[:, :]).then_inc(pcm, 16)

            @blk.gpsimd
            def _(gp):
                for c in range(KD):
                    gp.dma_start(wqT[:, c, :], fwT_d[128 * c:128 * (c + 1), :]).then_inc(pdq, 16)
                gp.memset(wdb[:, :, :], 0.0).then_inc(pg, 1)
                gp.memset(bdb[0:1, :], 0.0).then_inc(pg, 1)
                gp.memset(ones2b[:, :], 1.0)
                gp.memset(d_tok[:, :, 0:W], 0.0).then_inc(pg, 1)
                gp.memset(b_tok[:, :, 0:W], 0.0).then_inc(pg, 1)
                gp.memset(h_tok[:, :, 0:W], 0.0).then_inc(pg, 1)
                gp.memset(ones_r[0:1, :], 1.0).then_inc(pg, 1)
                gp.memset(ones1b[0:1, :], 1.0).then_inc(pg, 1)
                gp.iota(iog[:, :], [[0, 128]], channel_multiplier=1).then_inc(pg, 1)
                gp.iota(iof[:, :], [[1, 128]], channel_multiplier=0).then_inc(pg, 1)
                gp.memset(eps_t[:, :], EPS).then_inc(pg, 1)

            @blk.vector
            def _(v):
                v.wait_ge(pd2a, 32)
                v.tensor_copy(fb_hi[0:1, :], fbf[0:1, :]).then_inc(pv, 1)
                v.wait_ge(pv, 1)
                v.tensor_tensor(fb_lo[0:1, :], fbf[0:1, :], fb_hi[0:1, :], ALU.subtract).then_inc(pv, 1)
                v.wait_ge(pg, 9)
                v.tensor_tensor(ident[:, :], iog[:, :], iof[:, :], ALU.is_equal).then_inc(pv, 1)
                v.wait_ge(pv, 3)
                v.tensor_copy(ident8[:, :], ident[:, :]).then_inc(pv, 1)
                v.wait_ge(pds, pdn[0])
                v.tensor_copy(cm_hi[:, :], cmat[:, :]).then_inc(pv, 1)
                v.wait_ge(pv, 5)
                v.tensor_tensor(cm_lo[:, :], cmat[:, :], cm_hi[:, :], ALU.subtract).then_inc(pv, 1)
                v.wait_ge(pfb, 32)
                v.wait_ge(pd2b, pd2n[0])

        prep_st.close()

        NBX = 4
        NBP = 4
        xns = sbuf("xns", [128, NBP, D])
        xn2s = sbuf("xn2s", [128, NBI, D])
        x2T = sbuf("x2T", [128, 3, KD, 128])
        part = sbuf("part", [128, NB, D])
        s_tmp = sbuf("s_tmp", [128, 3, D])
        bn6a = sbuf("bn6a", [128, 12])
        nm2t = sbuf("nm2t", [128, 2])
        junk_v = sbuf("junk_v", [128, 16])
        junk_s = sbuf("junk_s", [128, 16])
        junk_g = sbuf("junk_g", [128, 16])
        ve1 = sbuf("ve1", [128, 1])
        ve2 = sbuf("ve2", [128, 1])
        rv1 = sbuf("rv1", [128, NT])
        rv2 = sbuf("rv2", [128, NT])
        mv1a = sbuf("mv1a", [128, NT, 2])
        mv2a = sbuf("mv2a", [128, NT, 2])
        r1a = sbuf("r1a", [128, NT])
        r2a = sbuf("r2a", [128, NT])
        hdb = sbuf("hdb", [S, 2, 2, LB])

        sx = [sem(f"sx{j}") for j in range(NBX)]
        spo = [sem(f"spo{j}") for j in range(NB)]
        v1 = sem("v1"); s1 = sem("s1"); t1 = sem("t1"); g1 = sem("g1")
        v2s = sem("v2s"); s2s = sem("s2s"); t2s = sem("t2s")
        spx = [sem(f"spx{j}") for j in range(3)]
        sxt = [sem(f"sxt{j}") for j in range(3)]
        spi = [sem(f"spi{j}") for j in range(NBP)]
        so3 = [sem(f"so3_{j}") for j in range(NB)]
        v3 = sem("v3"); s3 = sem("s3"); t3 = sem("t3"); g3 = sem("g3")

        jvn = [0]; jsn = [0]; jgn = [0]

        def _fill(eng, semh, n, junk, ctr):
            if junk is None:
                eng.ldweights(ones1b[0:1, 0:1]).then_inc(semh, n)
                return
            c = ctr[0] % 16; ctr[0] += 1
            if junk is junk_s:
                eng.copy(junk[:, c:c + 1], eps_t[:, :]).then_inc(semh, n)
            else:
                eng.tensor_copy(junk[:, c:c + 1], eps_t[:, :]).then_inc(semh, n)


        nV, nS, nT, nG = 13, 15, 35, 2
        WG0 = max(0, NT - 12)
        p1_st = ExitStack()
        xs = p1_st.enter_context(nc.sbuf_tensor("xs", [128, NBX, D], F32))
        wgf_r = p1_st.enter_context(nc.sbuf_tensor("wgf_r", [128, 2, D], F32))
        xq = p1_st.enter_context(nc.sbuf_tensor("xq", [128, NB, D], FP8))
        xqT = p1_st.enter_context(nc.sbuf_tensor("xqT", [128, NB, KD, 128], FP8))
        ps_sh = [p1_st.enter_context(nc.psum_tensor(f"ps_sh{i}", [128, D], F32)) for i in range(NB)]
        ps_db = p1_st.enter_context(nc.psum_tensor("ps_db", [64, NB, 128], F32))
        ps_q8 = p1_st.enter_context(nc.psum_tensor("ps_q8", [128, 2 * D], FP8))
        ps_ffn = p1_st.enter_context(nc.psum_tensor("ps_ffn", [128, D], F32))

        with nc.Block() as blk:
            @blk.sync
            def _(sync):
                for i in range(NT + 6):
                    j = i - 6
                    if 0 <= j < NT:
                        sync.wait_ge(v1, (j + 4) * nV + 13)
                        sync.dma_start(out_d[128 * j:128 * (j + 1), :], part[:, j % NB, :]).then_inc(spo[j % NB], 16)
                    j2 = i - 5
                    if 0 <= j2 < NT:
                        sync.wait_ge(s1, (i - 2) * nS + 12)
                        sync.dma_start(x2t_d[128 * j2:128 * (j2 + 1), :],
                                       x2T[:, j2 % 3, :, :]).then_inc(spx[j2 % 3], 16)
                    if i < NT:
                        if i >= NBX:
                            sync.wait_ge(g1, (i - 2) * nG + 2)
                            sync.wait_ge(v1, (i - 3) * nV + 7)
                        sync.dma_start(xs[:, i % NBX, :], x_d[128 * i:128 * (i + 1), :]).then_inc(sx[i % NBX], 16)
                for j in range(NB):
                    sync.wait_ge(spo[j], 16 * ((NT - j + NB - 1) // NB))

            @blk.vector
            def _(v):
                for i in range(NT + 4):
                    j = i - 2
                    if 0 <= j < NT:
                        v.wait_ge(s1, i * nS + 1)
                        v.tensor_scalar(nm2t[:, j % 2:j % 2 + 1], mv2a[:, j, 0:1], r2a[:, j:j + 1],
                                        -1.0, ALU.mult, ALU.mult).then_inc(v1, 1)
                    else:
                        _fill(v, v1, 1, junk_v, jvn)
                    if i < NT:
                        sl = i % NBX
                        v.wait_ge(sx[sl], 16 * (i // NBX + 1))
                        v.bn_stats(bn6a[:, 0:6], xs[:, sl, 0:N0]).then_inc(v1, 1)
                        v.bn_stats(bn6a[:, 6:12], xs[:, sl, N0:D]).then_inc(v1, 1)
                        v.wait_ge(v1, i * nV + 3)
                        v.bn_aggr(mv1a[:, i, :], bn6a[:, :]).then_inc(v1, 1)
                        v.wait_ge(v1, i * nV + 4)
                        v.tensor_scalar(ve1[:, :], mv1a[:, i, 1:2], EPS, None, ALU.add).then_inc(v1, 1)
                        v.wait_ge(v1, i * nV + 5)
                        v.reciprocal(rv1[:, i:i + 1], ve1[:, :]).then_inc(v1, 1)
                    else:
                        _fill(v, v1, 5, junk_v, jvn)
                    j = i - 1
                    if 0 <= j < NT:
                        sl = j % NBX
                        v.wait_ge(s1, i * nS + 2)
                        if j >= NBI:
                            v.wait_ge(g1, (i - 2) * nG + 2)
                            v.wait_ge(s1, (i - 2) * nS + 3)
                        v.tensor_scalar(xns[:, j % NBI, :], xs[:, sl, :], mv1a[:, j, 0:1], r1a[:, j:j + 1],
                                        ALU.subtract, ALU.mult).then_inc(v1, 1)
                        v.wait_ge(v1, i * nV + 7)
                        v.bn_stats(bn6a[:, 0:6], xns[:, j % NBI, 0:N0]).then_inc(v1, 1)
                        v.bn_stats(bn6a[:, 6:12], xns[:, j % NBI, N0:D]).then_inc(v1, 1)
                        v.wait_ge(v1, i * nV + 9)
                        v.bn_aggr(mv2a[:, j, :], bn6a[:, :]).then_inc(v1, 1)
                        v.wait_ge(v1, i * nV + 10)
                        v.tensor_scalar(ve2[:, :], mv2a[:, j, 1:2], EPS, None, ALU.add).then_inc(v1, 1)
                        v.wait_ge(v1, i * nV + 11)
                        v.reciprocal(rv2[:, j:j + 1], ve2[:, :]).then_inc(v1, 1)
                    else:
                        _fill(v, v1, 6, junk_v, jvn)
                    j = i - 4
                    if 0 <= j < NT:
                        v.wait_ge(t1, i * nT + 35)
                        v.wait_ge(g1, (i - 2) * nG + 2)
                        if j >= NB:
                            v.wait_ge(spo[j % NB], 16 * (j // NB))
                        v.tensor_add(part[:, j % NB, :], s_tmp[:, j % 3, :],
                                     ps_ffn[:, :]).then_inc(v1, 1)
                    else:
                        _fill(v, v1, 1, junk_v, jvn)

            @blk.scalar
            def _(sc):
                for i in range(NT + 4):
                    j = i - 2
                    if 0 <= j < NT:
                        sc.wait_ge(v1, (i - 1) * nV + 12)
                        sc.activation(r2a[:, j:j + 1], rv2[:, j:j + 1], AF.Sqrt).then_inc(s1, 1)
                    else:
                        _fill(sc, s1, 1, junk_s, jsn)
                    j = i - 1
                    if 0 <= j < NT:
                        sc.wait_ge(v1, (i - 1) * nV + 6)
                        sc.activation(r1a[:, j:j + 1], rv1[:, j:j + 1], AF.Sqrt).then_inc(s1, 1)
                    else:
                        _fill(sc, s1, 1, junk_s, jsn)
                    j = i - 2
                    if 0 <= j < NT:
                        sc.wait_ge(v1, (i - 1) * nV + 7)
                        if j >= NB:
                            sc.wait_ge(t1, (i - 1) * nT + 16)
                        sc.copy(xq[:, j % NB, 0:N0], xns[:, j % NBI, 0:N0]).then_inc(s1, 1)
                    else:
                        _fill(sc, s1, 1, junk_s, jsn)
                    j = i - 2
                    if 0 <= j < NT:
                        sc.wait_ge(v1, i * nV + 1)
                        if j >= NB:
                            sc.wait_ge(t1, (i - 2) * nT + 8)
                        sc.activation(xn2s[:, j % NBI, :], xns[:, j % NBI, :], AF.Identity,
                                      bias=nm2t[:, j % 2:j % 2 + 1],
                                      scale=r2a[:, j:j + 1]).then_inc(s1, 1)
                    else:
                        _fill(sc, s1, 1, junk_s, jsn)
                    j = i - 3
                    if 0 <= j < NT:
                        sc.wait_ge(t1, i * nT + 8)
                        if j >= 3:
                            sc.wait_ge(spx[j % 3], 16 * (j // 3))
                        for c in range(KD):
                            sc.activation(x2T[:, j % 3, c, :], ps_sh[j % NB][:, 128 * c:128 * (c + 1)],
                                          AF.Identity, bias=bet_s[:, c:c + 1],
                                          scale=gam_s[:, c:c + 1]).then_inc(s1, 1)
                        sc.wait_ge(t1, i * nT + 16)
                        sc.copy(xqT[:, j % NB, :, :].rearrange("p a b -> p (a b)"),
                                bass.AP(ps_q8, 0, [[2 * D, 128], [2, D]])).then_inc(s1, 1)
                    else:
                        _fill(sc, s1, 9, junk_s, jsn)
                    j = i - 4
                    if 0 <= j < NT:
                        sq = j // TPS
                        tr = (j % TPS) * 128
                        sc.wait_ge(t1, i * nT + 25)
                        sc.copy(d_tok[:, sq, W + tr:W + tr + 128],
                                ps_db[0:16, j % NB, :]).then_inc(s1, 1)
                        sc.copy(b_tok[:, sq, W + tr:W + tr + 128],
                                ps_db[32:48, j % NB, :]).then_inc(s1, 1)
                    else:
                        _fill(sc, s1, 2, junk_s, jsn)

            @blk.gpsimd
            def _(gp):
                for i in range(NT + 4):
                    j = i - 2
                    if 0 <= j < NT:
                        gp.wait_ge(v1, (i - 1) * nV + 7)
                        if j >= NB:
                            gp.wait_ge(t1, (i - 1) * nT + 16)
                        gp.tensor_scalar(xq[:, j % NB, N0:D], xns[:, j % NBI, N0:D], FP8MAX, -FP8MAX,
                                         ALU.min, ALU.max).then_inc(g1, 1)
                    else:
                        _fill(gp, g1, 1, junk_g, jgn)
                    j = i - 2
                    if 0 <= j < NT:
                        if j >= 3:
                            gp.wait_ge(v1, (i - 1) * nV + 13)
                        gp.tensor_add(s_tmp[:, j % 3, :], xs[:, j % NBX, :],
                                      xns[:, j % NBI, :]).then_inc(g1, 1)
                    else:
                        _fill(gp, g1, 1, junk_g, jgn)
                    if i == 0:
                        gp.dma_start(wg_hi[:, :, :],
                                     bass.AP(wg_d.tensor, 0, [[D, 128], [128 * D, KD], [1, D]])).then_inc(pdh, 16)
                        for c in range(2):
                            gp.dma_start(wgf_r[:, c, :],
                                         wg_d[128 * c:128 * (c + 1), :]).then_inc(pdw[c], 16)
                    c = i - WG0
                    if 0 <= c < KD:
                        if c == 0:
                            gp.wait_ge(pdh, 16)
                        gp.wait_ge(pdw[c % 2], 16 * (c // 2 + 1))
                        gp.tensor_tensor(wg_lo[:, c, :], wgf_r[:, c % 2, :],
                                         wg_hi[:, c, :], ALU.subtract).then_inc(pL, 1)
                        if c + 2 < KD:
                            gp.wait_ge(pL, c + 1)
                            gp.dma_start(wgf_r[:, c % 2, :],
                                         wg_d[128 * (c + 2):128 * (c + 3), :]).then_inc(pdw[c % 2], 16)

            @blk.tensor
            def _(t):
                for i in range(NT + 4):
                    j = i - 3
                    if 0 <= j < NT:
                        t.wait_ge(s1, (j + 2) * nS + 4)
                        if j >= NB:
                            t.wait_ge(s1, (i - 2) * nS + 12)
                        for c in range(KD):
                            t.transpose(ps_sh[j % NB][:, 128 * c:128 * (c + 1)],
                                        xn2s[:, j % NBI, 128 * c:128 * (c + 1)],
                                        ident[:, :]).then_inc(t1, 1)
                        t.wait_ge(s1, (i - 1) * nS + 3)
                        if j >= 1:
                            t.wait_ge(s1, (i - 1) * nS + 13)
                        for c in range(KD // 2):
                            t.transpose(bass.AP(ps_q8, 2 * 128 * c, [[2 * D, 128], [2, 128]]),
                                        xq[:, j % NB, 128 * c:128 * (c + 1)],
                                        ident8[:, :]).then_inc(t1, 1)
                        t.wait_ge(g1, (i - 1) * nG + 1)
                        for c in range(KD // 2, KD):
                            t.transpose(bass.AP(ps_q8, 2 * 128 * c, [[2 * D, 128], [2, 128]]),
                                        xq[:, j % NB, 128 * c:128 * (c + 1)],
                                        ident8[:, :]).then_inc(t1, 1)
                    else:
                        _fill(t, t1, 16, None, None)
                    j = i - 4
                    if 0 <= j < NT:
                        t.wait_ge(s1, (i - 1) * nS + 12)
                        if j >= 1:
                            t.wait_ge(s1, (i - 1) * nS + 15)
                        for c in range(KD):
                            t.matmul(ps_db[:, j % NB, :], wdb[:, c, :], x2T[:, j % 3, c, :],
                                     start=(c == 0), stop=False).then_inc(t1, 1)
                        t.matmul(ps_db[:, j % NB, :], bdb[0:1, :], ones_r[0:1, :],
                                 start=False, stop=True).then_inc(t1, 1)
                        t.wait_ge(s1, (i - 1) * nS + 13)
                        if j == 0:
                            t.wait_ge(pdq, 16 * KD)
                        if j >= 1:
                            t.wait_ge(v1, (i - 1) * nV + 13)
                        for h in range(2):
                            for c in range(0, KD, 2):
                                t.matmul(ps_ffn[:, h * N0:(h + 1) * N0], xqT[:, j % NB, c:c + 2, :],
                                         wqT[:, c:c + 2, h * N0:(h + 1) * N0],
                                         start=(c == 0), stop=False,
                                         perf_mode=mybir.MatmulPerfMode.DoubleRow).then_inc(t1, 1)
                            t.matmul(ps_ffn[:, h * N0:(h + 1) * N0], ones2b[0:2, :],
                                     fb2[0:2, h * N0:(h + 1) * N0], start=False, stop=True).then_inc(t1, 1)
                    else:
                        _fill(t, t1, 19, None, None)

        p1_st.close()

        p2_st = ExitStack()
        ps_sc = [p2_st.enter_context(nc.psum_tensor(f"ps_sc{i}", [S, LB], F32)) for i in range(4)]

        def ap_tok(tens, g, col, n, step):
            return bass.AP(tens, g * WT + col, [[n_seq * WT, S], [step, n]])

        with nc.Block() as blk:
            @blk.scalar
            def _(sc):
                sc.activation(d_tok[:, :, :], d_tok[:, :, :], AF.Sigmoid).then_inc(s2s, 1)
                for i in range(NSTEP):
                    for g in range(n_seq):
                        if i == 0:
                            sc.wait_ge(t2s, 1 + g)
                        else:
                            sc.wait_ge(t2s, 2 + 4 * (i - 1) + 2 + 2 * g)
                        sc.activation(ap_tok(h_tok, g, i, LB, L),
                                      ps_sc[2 * g + i % 2][:, :], AF.Tanh).then_inc(s2s, 1)

            @blk.vector
            def _(v):
                for i in range(1, NSTEP):
                    for g in range(n_seq):
                        v.wait_ge(s2s, 1 + 2 * (i - 1) + g + 1)
                        v.tensor_mul(hdb[:, i % 2, g, :],
                                     ap_tok(h_tok, g, i - 1, LB, L),
                                     ap_tok(d_tok, g, i, LB, L)).then_inc(v2s, 1)

            @blk.tensor
            def _(t):
                t.wait_ge(s2s, 1)
                t.matmul(ps_sc[0][:, :], ident[0:S, 0:S], ap_tok(b_tok, 0, 0, LB, L),
                         start=True, stop=True).then_inc(t2s, 1)
                t.matmul(ps_sc[2][:, :], ident[0:S, 0:S], ap_tok(b_tok, 1, 0, LB, L),
                         start=True, stop=True).then_inc(t2s, 1)
                for i in range(1, NSTEP):
                    for g in range(n_seq):
                        if i >= 2:
                            t.wait_ge(s2s, 1 + 2 * (i - 2) + g + 1)
                        t.matmul(ps_sc[2 * g + i % 2][:, :], ident[0:S, 0:S],
                                 ap_tok(b_tok, g, i, LB, L),
                                 start=True, stop=False).then_inc(t2s, 1)
                        t.wait_ge(v2s, 2 * (i - 1) + g + 1)
                        t.matmul(ps_sc[2 * g + i % 2][:, :], a16[:, :], hdb[:, i % 2, g, :],
                                 start=False, stop=True).then_inc(t2s, 1)

        p2_st.close()

        nV3, nS3, nT3, nG3 = 8, 2, 54, 1
        p3_st = ExitStack()
        t3hi = p3_st.enter_context(nc.sbuf_tensor("t3hi", [128, 3, KD, 128], BF16))
        t3lo = p3_st.enter_context(nc.sbuf_tensor("t3lo", [128, 3, KD, 128], BF16))
        gsb = p3_st.enter_context(nc.sbuf_tensor("gsb", [128, NB, D], F32))
        hs3 = p3_st.enter_context(nc.sbuf_tensor("hs3", [3 * S, NBP, 128], BF16))
        hs_tl = p3_st.enter_context(nc.sbuf_tensor("hs_tl", [S, NBP, 128], BF16))
        ps_sh3 = [p3_st.enter_context(nc.psum_tensor(f"ps_sh3{i}", [128, D], F32)) for i in range(3)]
        ps_g = [p3_st.enter_context(nc.psum_tensor(f"ps_g{i}", [128, N0], F32)) for i in range(2)]

        with nc.Block() as blk:
            @blk.sync
            def _(sync):
                for k in range(NT + 5):
                    j = k - 5
                    if 0 <= j < NT:
                        sync.wait_ge(g3, (j + 4) * nG3)
                        sync.dma_start(out_d[128 * j:128 * (j + 1), :], part[:, j % NB, :]).then_inc(so3[j % NB], 16)
                    if k < NT:
                        if k < 3:
                            sync.wait_ge(spx[k], 16 * ((NT - k + 2) // 3))
                        else:
                            sync.wait_ge(v3, (k - 1) * nV3 + 2)
                        sync.dma_start(x2T[:, k % 3, :, :],
                                       x2t_d[128 * k:128 * (k + 1), :]).then_inc(sxt[k % 3], 16)
                        if k >= NBP:
                            sync.wait_ge(g3, k * nG3)
                        sync.dma_start(xns[:, k % NBP, :], out_d[128 * k:128 * (k + 1), :]).then_inc(spi[k % NBP], 16)
                        sync.wait_ge(v3, k * nV3 + 8)
                        sync.dma_start(hs3[S:2 * S, k % NBP, :], hs3[0:S, k % NBP, :]).then_inc(sdh[k % NBP], 16)
                        sync.dma_start(hs3[2 * S:3 * S, k % NBP, :], hs_tl[:, k % NBP, :]).then_inc(sdh[k % NBP], 16)
                for j in range(NB):
                    sync.wait_ge(so3[j], 16 * ((NT - j + NB - 1) // NB))

            @blk.vector
            def _(v):
                for k in range(NT + 3):
                    j = k - 2
                    if 0 <= j < NT:
                        v.wait_ge(sxt[j % 3], 16 * (j // 3 + 1))
                        if j >= 3:
                            v.wait_ge(t3, (k - 2) * nT3 + 40)
                        v.tensor_copy(t3hi[:, j % 3, :, :], x2T[:, j % 3, :, :]).then_inc(v3, 1)
                        v.wait_ge(v3, k * nV3 + 1)
                        if j >= 3:
                            v.wait_ge(t3, (k - 2) * nT3 + 48)
                        v.tensor_tensor(t3lo[:, j % 3, :, :], x2T[:, j % 3, :, :], t3hi[:, j % 3, :, :],
                                        ALU.subtract).then_inc(v3, 1)
                    else:
                        _fill(v, v3, 2, junk_v, jvn)
                    j = k - 3
                    if 0 <= j < NT:
                        v.wait_ge(t3, k * nT3 + 24)
                        if j >= NB:
                            v.wait_ge(s3, (k - 2) * nS3 + 1)
                        v.tensor_add(gsb[:, j % NB, 0:N0], ps_g[0][:, :], bgate_bc[:, 0:N0]).then_inc(v3, 1)
                        v.wait_ge(t3, k * nT3 + 48)
                        if j >= NB:
                            v.wait_ge(s3, (k - 2) * nS3 + 2)
                        v.tensor_add(gsb[:, j % NB, N0:D], ps_g[1][:, :], bgate_bc[:, N0:D]).then_inc(v3, 1)
                        v.wait_ge(t3, k * nT3 + 54)
                        v.wait_ge(s3, k * nS3 + 1)
                        if k >= 4:
                            v.wait_ge(g3, k * nG3)
                        v.tensor_mul(s_tmp[:, j % NB, 0:N0], ps_sh3[j % 3][:, 0:N0],
                                     gsb[:, j % NB, 0:N0]).then_inc(v3, 1)
                        v.wait_ge(s3, k * nS3 + 2)
                        v.tensor_mul(s_tmp[:, j % NB, N0:D], ps_sh3[j % 3][:, N0:D],
                                     gsb[:, j % NB, N0:D]).then_inc(v3, 1)
                    else:
                        _fill(v, v3, 4, junk_v, jvn)
                    if k < NT:
                        sq3 = k // TPS
                        tr3 = (k % TPS) * 128
                        if k == 0:
                            v.wait_ge(s2s, 1 + 2 * NSTEP)
                        if k >= NBP:
                            v.wait_ge(t3, (k - 1) * nT3 + 54)
                        v.tensor_copy(hs3[0:S, k % NBP, :],
                                      h_tok[:, sq3, W + tr3:W + tr3 + 128]).then_inc(v3, 1)
                        v.wait_ge(v3, k * nV3 + 7)
                        v.tensor_tensor(hs_tl[:, k % NBP, :],
                                        h_tok[:, sq3, W + tr3:W + tr3 + 128],
                                        hs3[0:S, k % NBP, :], ALU.subtract).then_inc(v3, 1)
                    else:
                        _fill(v, v3, 2, junk_v, jvn)

            @blk.scalar
            def _(sc):
                for k in range(NT + 3):
                    j = k - 3
                    if 0 <= j < NT:
                        sc.wait_ge(v3, k * nV3 + 3)
                        sc.activation(gsb[:, j % NB, 0:N0], gsb[:, j % NB, 0:N0], AF.Sigmoid).then_inc(s3, 1)
                        sc.wait_ge(v3, k * nV3 + 4)
                        sc.activation(gsb[:, j % NB, N0:D], gsb[:, j % NB, N0:D], AF.Sigmoid).then_inc(s3, 1)
                    else:
                        _fill(sc, s3, 2, junk_s, jsn)

            @blk.gpsimd
            def _(gp):
                for k in range(NT + 3):
                    j = k - 3
                    if 0 <= j < NT:
                        gp.wait_ge(v3, k * nV3 + 6)
                        gp.wait_ge(spi[j % NBP], 16 * (j // NBP + 1))
                        if j >= NB:
                            gp.wait_ge(so3[j % NB], 16 * ((j - NB) // NB + 1))
                        gp.tensor_add(part[:, j % NB, :], s_tmp[:, j % NB, :],
                                      xns[:, j % NBP, :]).then_inc(g3, 1)
                    else:
                        _fill(gp, g3, 1, junk_g, jgn)

            @blk.tensor
            def _(t):
                for k in range(NT + 3):
                    j = k - 3
                    if 0 <= j < NT:
                        t.wait_ge(v3, (k - 1) * nV3 + 2)
                        if j == 0:
                            t.wait_ge(pL, KD)
                        for h in range(2):
                            if j >= 1:
                                t.wait_ge(v3, (k - 1) * nV3 + 3 + h)
                            for c in range(KD):
                                t.matmul(ps_g[h][:, :], t3hi[:, j % 3, c, :],
                                         wg_hi[:, c, h * N0:(h + 1) * N0],
                                         start=(c == 0), stop=False)
                                mm = t.matmul(ps_g[h][:, :], t3hi[:, j % 3, c, :],
                                              wg_lo[:, c, h * N0:(h + 1) * N0],
                                              start=False, stop=False)
                                if h == 1 and c == KD - 1:
                                    mm.then_inc(t3, 16)
                            for c in range(KD):
                                mm = t.matmul(ps_g[h][:, :], t3lo[:, j % 3, c, :],
                                              wg_hi[:, c, h * N0:(h + 1) * N0],
                                              start=False, stop=(c == KD - 1))
                                if c == KD - 1:
                                    mm.then_inc(t3, 24 if h == 0 else 8)
                        t.wait_ge(v3, (k - 3) * nV3 + 8)
                        t.wait_ge(sdh[j % NBP], 32 * (j // NBP + 1))
                        if j == 0:
                            t.wait_ge(pcm, 48)
                        for h in range(2):
                            t.matmul(ps_sh3[j % 3][:, h * N0:(h + 1) * N0], hs3[:, j % NBP, :],
                                     cm3[:, h * N0:(h + 1) * N0], start=True, stop=True)
                        t.ldweights(ones1b[0:1, 0:1]).then_inc(t3, 6)
                    else:
                        _fill(t, t3, 54, None, None)
        p3_st.close()

    return nc


def kernel(L0_attn_x, L0_attn_ln_gamma, L0_attn_ln_beta, L0_attn_A, L0_attn_C,
           L0_attn_w_delta, L0_attn_b_delta, L0_attn_w_in, L0_attn_b_in,
           L0_attn_w_gate, L0_attn_b_gate, L0_ffn_weight, L0_ffn_bias):
    B, T, _ = L0_attn_x.shape
    n_cores = 8
    spc = B // n_cores
    x = np.ascontiguousarray(np.asarray(L0_attn_x, dtype=np.float32))
    common = {
        "w_gate": np.ascontiguousarray(np.asarray(L0_attn_w_gate, np.float32)),
        "w_delta": np.ascontiguousarray(np.asarray(L0_attn_w_delta, np.float32)),
        "w_in": np.ascontiguousarray(np.asarray(L0_attn_w_in, np.float32)),
        "b_gate": np.asarray(L0_attn_b_gate, np.float32).reshape(1, -1),
        "b_delta": np.asarray(L0_attn_b_delta, np.float32).reshape(1, -1),
        "b_in": np.asarray(L0_attn_b_in, np.float32).reshape(1, -1),
        "ln_gamma": np.asarray(L0_attn_ln_gamma, np.float32).reshape(1, -1),
        "ln_beta": np.asarray(L0_attn_ln_beta, np.float32).reshape(1, -1),
        "a_mat": np.ascontiguousarray(np.asarray(L0_attn_A, np.float32)),
        "c_mat": np.ascontiguousarray(np.asarray(L0_attn_C, np.float32)),
        "ffn_wT": np.ascontiguousarray(np.asarray(L0_ffn_weight, np.float32).T),
        "ffn_b": np.asarray(L0_ffn_bias, np.float32).reshape(1, -1),
    }
    in_maps = []
    for c in range(n_cores):
        m = dict(common)
        m["x"] = np.ascontiguousarray(x[c * spc:(c + 1) * spc].reshape(spc * T, -1))
        in_maps.append(m)
    nc = build_nc(n_seq=spc, T=T)
    res = run_bass_kernel_spmd(nc, in_maps, core_ids=list(range(n_cores)))
    outs = [np.asarray(r["out"]).reshape(spc, T, -1) for r in res.results]
    return np.concatenate(outs, axis=0)


# version 62
# speedup vs baseline: 1.0284x; 1.0042x over previous
import numpy as np
from contextlib import ExitStack

import concourse.bass as bass
import concourse.mybir as mybir
from concourse.bass_utils import run_bass_kernel_spmd

F32 = mybir.dt.float32
BF16 = mybir.dt.bfloat16
FP8 = mybir.dt.float8e4
AF = mybir.ActivationFunctionType
ALU = mybir.AluOpType

D = 1024
KD = 8
S = 16
EPS = 1e-5
FP8MAX = 448.0
N0 = 512


def build_nc(n_seq=2, T=2048, C=128, W=16, debug=False, skip_p3_store=False):
    L = T // C
    assert W <= L and T % C == 0
    NSTEP = W + L
    LB = C
    WT = W + T
    NTOK = n_seq * T
    NT = NTOK // 128
    TPS = T // 128
    NB = 2
    NBI = 3

    nc = bass.Bass("TRN2", target_bir_lowering=False, debug=debug)

    x_d = nc.dram_tensor("x", [NTOK, D], F32, kind="ExternalInput").ap()
    wg_d = nc.dram_tensor("w_gate", [D, D], F32, kind="ExternalInput").ap()
    wdel_d = nc.dram_tensor("w_delta", [D, S], F32, kind="ExternalInput").ap()
    win_d = nc.dram_tensor("w_in", [D, S], F32, kind="ExternalInput").ap()
    bg_d = nc.dram_tensor("b_gate", [1, D], F32, kind="ExternalInput").ap()
    bdel_d = nc.dram_tensor("b_delta", [1, S], F32, kind="ExternalInput").ap()
    bin_d = nc.dram_tensor("b_in", [1, S], F32, kind="ExternalInput").ap()
    gam_d = nc.dram_tensor("ln_gamma", [1, D], F32, kind="ExternalInput").ap()
    bet_d = nc.dram_tensor("ln_beta", [1, D], F32, kind="ExternalInput").ap()
    amat_d = nc.dram_tensor("a_mat", [S, S], F32, kind="ExternalInput").ap()
    cmat_d = nc.dram_tensor("c_mat", [S, D], F32, kind="ExternalInput").ap()
    fwT_d = nc.dram_tensor("ffn_wT", [D, D], F32, kind="ExternalInput").ap()
    fb_d = nc.dram_tensor("ffn_b", [1, D], F32, kind="ExternalInput").ap()
    out_d = nc.dram_tensor("out", [NTOK, D], F32, kind="ExternalOutput").ap()
    x2t_d = nc.dram_tensor("x2t_scratch", [NTOK, D], F32).ap()

    st = ExitStack()
    sbuf = lambda n, sh, dt=F32: st.enter_context(nc.sbuf_tensor(n, sh, dt))
    sem = lambda n: st.enter_context(nc.semaphore(n))

    with st:
        wg_hi = sbuf("wg_hi", [128, KD, D], BF16)
        wg_lo = sbuf("wg_lo", [128, KD, D], BF16)
        wdb = sbuf("wdb", [128, KD, 64])
        bdb = sbuf("bdb", [1, 64])
        fb2 = sbuf("fb2", [2, D], BF16)
        ones2b = sbuf("ones2b", [2, 128], BF16)
        cm_hi = sbuf("cm_hi", [S, D], BF16)
        cm_lo = sbuf("cm_lo", [S, D], BF16)
        cm3 = sbuf("cm3", [3 * S, D], BF16)
        ones_r = sbuf("ones_r", [1, 128])
        ones1b = sbuf("ones1b", [1, 128], BF16)
        bgate_bc = sbuf("bgate_bc", [128, D])
        fb_hi = sbuf("fb_hi", [1, D], BF16)
        fb_lo = sbuf("fb_lo", [1, D], BF16)
        wqT = sbuf("wqT", [128, KD, D], FP8)
        a16 = sbuf("a16", [S, S])
        gam_s = sbuf("gam_s", [128, KD])
        bet_s = sbuf("bet_s", [128, KD])
        ident = sbuf("ident", [128, 128])
        ident8 = sbuf("ident8", [128, 128], FP8)
        eps_t = sbuf("eps_t", [128, 1])
        d_tok = sbuf("d_tok", [S, n_seq, WT])
        b_tok = sbuf("b_tok", [S, n_seq, WT])
        h_tok = sbuf("h_tok", [S, n_seq, WT])

        prep_st = ExitStack()
        iog = prep_st.enter_context(nc.sbuf_tensor("iog", [128, 128], mybir.dt.int32))
        iof = prep_st.enter_context(nc.sbuf_tensor("iof", [128, 128], mybir.dt.int32))
        bgf = prep_st.enter_context(nc.sbuf_tensor("bgf", [1, D], F32))
        cmat = prep_st.enter_context(nc.sbuf_tensor("cmat", [S, D], F32))
        fbf = prep_st.enter_context(nc.sbuf_tensor("fbf", [1, D], F32))

        pdq = sem("pdq"); pdh = sem("pdh"); pL = sem("pL"); pcm = sem("pcm")
        sdh = [sem(f"sdh{j}") for j in range(4)]
        pdw = [sem(f"pdw{j}") for j in range(2)]
        pd2a = sem("pd2a"); pd2b = sem("pd2b")
        pg = sem("pg"); pv = sem("pv"); pds = sem("pds"); pfb = sem("pfb")

        with nc.Block() as blk:
            pdn = [0]; pd2n = [0]

            @blk.sync
            def _(sync):
                for c in range(KD):
                    sync.dma_start(gam_s[:, c:c + 1], gam_d[0:1, 128 * c:128 * (c + 1)]).then_inc(pds, 16); pdn[0] += 16
                for c in range(KD):
                    sync.dma_start(bet_s[:, c:c + 1], bet_d[0:1, 128 * c:128 * (c + 1)]).then_inc(pds, 16); pdn[0] += 16
                sync.dma_start(a16[:, :], amat_d[:, :]).then_inc(pds, 16); pdn[0] += 16
                sync.dma_start(cmat[:, :], cmat_d[:, :]).then_inc(pds, 16); pdn[0] += 16

            @blk.scalar
            def _(sc):
                sc.dma_start(fbf[0:1, :], fb_d[0:1, :]).then_inc(pd2a, 16)
                sc.dma_start(bgate_bc[:, :], bass.AP(bg_d.tensor, 0, [[0, 128], [1, D]])).then_inc(pd2a, 16)
                sc.wait_ge(pg, 2)
                for c in range(KD):
                    sc.dma_start(wdb[:, c, 0:S], wdel_d[128 * c:128 * (c + 1), :]).then_inc(pd2b, 16); pd2n[0] += 16
                for c in range(KD):
                    sc.dma_start(wdb[:, c, 32:32 + S], win_d[128 * c:128 * (c + 1), :]).then_inc(pd2b, 16); pd2n[0] += 16
                sc.dma_start(bdb[0:1, 0:S], bdel_d[0:1, :]).then_inc(pd2b, 16); pd2n[0] += 16
                sc.dma_start(bdb[0:1, 32:32 + S], bin_d[0:1, :]).then_inc(pd2b, 16); pd2n[0] += 16
                sc.wait_ge(pv, 2)
                sc.dma_start(fb2[0:1, :], fb_hi[0:1, :]).then_inc(pfb, 16)
                sc.dma_start(fb2[1:2, :], fb_lo[0:1, :]).then_inc(pfb, 16)
                sc.wait_ge(pv, 6)
                sc.dma_start(cm3[0:S, :], cm_hi[:, :]).then_inc(pcm, 16)
                sc.dma_start(cm3[S:2 * S, :], cm_lo[:, :]).then_inc(pcm, 16)
                sc.dma_start(cm3[2 * S:3 * S, :], cm_hi[:, :]).then_inc(pcm, 16)

            @blk.gpsimd
            def _(gp):
                for c in range(KD):
                    gp.dma_start(wqT[:, c, :], fwT_d[128 * c:128 * (c + 1), :]).then_inc(pdq, 16)
                gp.memset(wdb[:, :, :], 0.0).then_inc(pg, 1)
                gp.memset(bdb[0:1, :], 0.0).then_inc(pg, 1)
                gp.memset(ones2b[:, :], 1.0)
                gp.memset(d_tok[:, :, 0:W], 0.0).then_inc(pg, 1)
                gp.memset(b_tok[:, :, 0:W], 0.0).then_inc(pg, 1)
                gp.memset(h_tok[:, :, 0:W], 0.0).then_inc(pg, 1)
                gp.memset(ones_r[0:1, :], 1.0).then_inc(pg, 1)
                gp.memset(ones1b[0:1, :], 1.0).then_inc(pg, 1)
                gp.iota(iog[:, :], [[0, 128]], channel_multiplier=1).then_inc(pg, 1)
                gp.iota(iof[:, :], [[1, 128]], channel_multiplier=0).then_inc(pg, 1)
                gp.memset(eps_t[:, :], EPS).then_inc(pg, 1)

            @blk.vector
            def _(v):
                v.wait_ge(pd2a, 32)
                v.tensor_copy(fb_hi[0:1, :], fbf[0:1, :]).then_inc(pv, 1)
                v.wait_ge(pv, 1)
                v.tensor_tensor(fb_lo[0:1, :], fbf[0:1, :], fb_hi[0:1, :], ALU.subtract).then_inc(pv, 1)
                v.wait_ge(pg, 9)
                v.tensor_tensor(ident[:, :], iog[:, :], iof[:, :], ALU.is_equal).then_inc(pv, 1)
                v.wait_ge(pv, 3)
                v.tensor_copy(ident8[:, :], ident[:, :]).then_inc(pv, 1)
                v.wait_ge(pds, pdn[0])
                v.tensor_copy(cm_hi[:, :], cmat[:, :]).then_inc(pv, 1)
                v.wait_ge(pv, 5)
                v.tensor_tensor(cm_lo[:, :], cmat[:, :], cm_hi[:, :], ALU.subtract).then_inc(pv, 1)
                v.wait_ge(pfb, 32)
                v.wait_ge(pd2b, pd2n[0])

        prep_st.close()

        NBX = 4
        NBP = 4
        xns = sbuf("xns", [128, NBP, D])
        xn2s = sbuf("xn2s", [128, NBI, D])
        x2T = sbuf("x2T", [128, 3, KD, 128])
        part = sbuf("part", [128, NB, D])
        s_tmp = sbuf("s_tmp", [128, 3, D])
        bn6a = sbuf("bn6a", [128, 12])
        nm2t = sbuf("nm2t", [128, 2])
        junk_v = sbuf("junk_v", [128, 16])
        junk_s = sbuf("junk_s", [128, 16])
        junk_g = sbuf("junk_g", [128, 16])
        ve1 = sbuf("ve1", [128, 1])
        ve2 = sbuf("ve2", [128, 1])
        rv1 = sbuf("rv1", [128, NT])
        rv2 = sbuf("rv2", [128, NT])
        mv1a = sbuf("mv1a", [128, NT, 2])
        mv2a = sbuf("mv2a", [128, NT, 2])
        r1a = sbuf("r1a", [128, NT])
        r2a = sbuf("r2a", [128, NT])
        hdb = sbuf("hdb", [S, 2, 2, LB])

        sx = [sem(f"sx{j}") for j in range(NBX)]
        spo = [sem(f"spo{j}") for j in range(NB)]
        v1 = sem("v1"); s1 = sem("s1"); t1 = sem("t1"); g1 = sem("g1")
        v2s = sem("v2s"); s2s = sem("s2s"); t2s = sem("t2s")
        spx = [sem(f"spx{j}") for j in range(3)]
        sxt = [sem(f"sxt{j}") for j in range(3)]
        spi = [sem(f"spi{j}") for j in range(NBP)]
        so3 = [sem(f"so3_{j}") for j in range(NB)]
        v3 = sem("v3"); s3 = sem("s3"); t3 = sem("t3"); g3 = sem("g3")

        jvn = [0]; jsn = [0]; jgn = [0]

        def _fill(eng, semh, n, junk, ctr):
            if junk is None:
                eng.ldweights(ones1b[0:1, 0:1]).then_inc(semh, n)
                return
            c = ctr[0] % 16; ctr[0] += 1
            if junk is junk_s:
                eng.copy(junk[:, c:c + 1], eps_t[:, :]).then_inc(semh, n)
            else:
                eng.tensor_copy(junk[:, c:c + 1], eps_t[:, :]).then_inc(semh, n)


        nV, nS, nT, nG = 13, 15, 35, 2
        WG0 = max(0, NT - 12)
        p1_st = ExitStack()
        xs = p1_st.enter_context(nc.sbuf_tensor("xs", [128, NBX, D], F32))
        wgf_r = p1_st.enter_context(nc.sbuf_tensor("wgf_r", [128, 2, D], F32))
        xq = p1_st.enter_context(nc.sbuf_tensor("xq", [128, NB, D], FP8))
        xqT = p1_st.enter_context(nc.sbuf_tensor("xqT", [128, NB, KD, 128], FP8))
        ps_sh = [p1_st.enter_context(nc.psum_tensor(f"ps_sh{i}", [128, D], F32)) for i in range(NB)]
        ps_db = p1_st.enter_context(nc.psum_tensor("ps_db", [64, NB, 128], F32))
        ps_q8 = p1_st.enter_context(nc.psum_tensor("ps_q8", [128, 2 * D], FP8))
        ps_ffn = p1_st.enter_context(nc.psum_tensor("ps_ffn", [128, D], F32))

        with nc.Block() as blk:
            @blk.sync
            def _(sync):
                for i in range(NT + 6):
                    j = i - 6
                    if 0 <= j < NT:
                        sync.wait_ge(v1, (j + 4) * nV + 13)
                        sync.dma_start(out_d[128 * j:128 * (j + 1), :], part[:, j % NB, :]).then_inc(spo[j % NB], 16)
                    j2 = i - 5
                    if 0 <= j2 < NT:
                        sync.wait_ge(s1, (i - 2) * nS + 12)
                        sync.dma_start(x2t_d[128 * j2:128 * (j2 + 1), :],
                                       x2T[:, j2 % 3, :, :]).then_inc(spx[j2 % 3], 16)
                    if i < NT:
                        if i >= NBX:
                            sync.wait_ge(g1, (i - 2) * nG + 2)
                            sync.wait_ge(v1, (i - 3) * nV + 7)
                        sync.dma_start(xs[:, i % NBX, :], x_d[128 * i:128 * (i + 1), :]).then_inc(sx[i % NBX], 16)
                for j in range(NB):
                    sync.wait_ge(spo[j], 16 * ((NT - j + NB - 1) // NB))

            @blk.vector
            def _(v):
                for i in range(NT + 4):
                    j = i - 2
                    if 0 <= j < NT:
                        v.wait_ge(s1, i * nS + 1)
                        v.tensor_scalar(nm2t[:, j % 2:j % 2 + 1], mv2a[:, j, 0:1], r2a[:, j:j + 1],
                                        -1.0, ALU.mult, ALU.mult).then_inc(v1, 1)
                    else:
                        _fill(v, v1, 1, junk_v, jvn)
                    if i < NT:
                        sl = i % NBX
                        v.wait_ge(sx[sl], 16 * (i // NBX + 1))
                        v.bn_stats(bn6a[:, 0:6], xs[:, sl, 0:N0]).then_inc(v1, 1)
                        v.bn_stats(bn6a[:, 6:12], xs[:, sl, N0:D]).then_inc(v1, 1)
                        v.wait_ge(v1, i * nV + 3)
                        v.bn_aggr(mv1a[:, i, :], bn6a[:, :]).then_inc(v1, 1)
                        v.wait_ge(v1, i * nV + 4)
                        v.tensor_scalar(ve1[:, :], mv1a[:, i, 1:2], EPS, None, ALU.add).then_inc(v1, 1)
                        v.wait_ge(v1, i * nV + 5)
                        v.reciprocal(rv1[:, i:i + 1], ve1[:, :]).then_inc(v1, 1)
                    else:
                        _fill(v, v1, 5, junk_v, jvn)
                    j = i - 1
                    if 0 <= j < NT:
                        sl = j % NBX
                        v.wait_ge(s1, i * nS + 2)
                        if j >= NBI:
                            v.wait_ge(g1, (i - 2) * nG + 2)
                            v.wait_ge(s1, (i - 2) * nS + 3)
                        v.tensor_scalar(xns[:, j % NBI, :], xs[:, sl, :], mv1a[:, j, 0:1], r1a[:, j:j + 1],
                                        ALU.subtract, ALU.mult).then_inc(v1, 1)
                        v.wait_ge(v1, i * nV + 7)
                        v.bn_stats(bn6a[:, 0:6], xns[:, j % NBI, 0:N0]).then_inc(v1, 1)
                        v.bn_stats(bn6a[:, 6:12], xns[:, j % NBI, N0:D]).then_inc(v1, 1)
                        v.wait_ge(v1, i * nV + 9)
                        v.bn_aggr(mv2a[:, j, :], bn6a[:, :]).then_inc(v1, 1)
                        v.wait_ge(v1, i * nV + 10)
                        v.tensor_scalar(ve2[:, :], mv2a[:, j, 1:2], EPS, None, ALU.add).then_inc(v1, 1)
                        v.wait_ge(v1, i * nV + 11)
                        v.reciprocal(rv2[:, j:j + 1], ve2[:, :]).then_inc(v1, 1)
                    else:
                        _fill(v, v1, 6, junk_v, jvn)
                    j = i - 4
                    if 0 <= j < NT:
                        v.wait_ge(t1, i * nT + 35)
                        v.wait_ge(g1, (i - 2) * nG + 2)
                        if j >= NB:
                            v.wait_ge(spo[j % NB], 16 * (j // NB))
                        v.tensor_add(part[:, j % NB, :], s_tmp[:, j % 3, :],
                                     ps_ffn[:, :]).then_inc(v1, 1)
                    else:
                        _fill(v, v1, 1, junk_v, jvn)

            @blk.scalar
            def _(sc):
                for i in range(NT + 4):
                    j = i - 2
                    if 0 <= j < NT:
                        sc.wait_ge(v1, (i - 1) * nV + 12)
                        sc.activation(r2a[:, j:j + 1], rv2[:, j:j + 1], AF.Sqrt).then_inc(s1, 1)
                    else:
                        _fill(sc, s1, 1, junk_s, jsn)
                    j = i - 1
                    if 0 <= j < NT:
                        sc.wait_ge(v1, (i - 1) * nV + 6)
                        sc.activation(r1a[:, j:j + 1], rv1[:, j:j + 1], AF.Sqrt).then_inc(s1, 1)
                    else:
                        _fill(sc, s1, 1, junk_s, jsn)
                    j = i - 2
                    if 0 <= j < NT:
                        sc.wait_ge(v1, (i - 1) * nV + 7)
                        if j >= NB:
                            sc.wait_ge(t1, (i - 1) * nT + 16)
                        sc.copy(xq[:, j % NB, 0:N0], xns[:, j % NBI, 0:N0]).then_inc(s1, 1)
                    else:
                        _fill(sc, s1, 1, junk_s, jsn)
                    j = i - 2
                    if 0 <= j < NT:
                        sc.wait_ge(v1, i * nV + 1)
                        if j >= NB:
                            sc.wait_ge(t1, (i - 2) * nT + 8)
                        sc.activation(xn2s[:, j % NBI, :], xns[:, j % NBI, :], AF.Identity,
                                      bias=nm2t[:, j % 2:j % 2 + 1],
                                      scale=r2a[:, j:j + 1]).then_inc(s1, 1)
                    else:
                        _fill(sc, s1, 1, junk_s, jsn)
                    j = i - 3
                    if 0 <= j < NT:
                        sc.wait_ge(t1, i * nT + 8)
                        if j >= 3:
                            sc.wait_ge(spx[j % 3], 16 * (j // 3))
                        for c in range(KD):
                            sc.activation(x2T[:, j % 3, c, :], ps_sh[j % NB][:, 128 * c:128 * (c + 1)],
                                          AF.Identity, bias=bet_s[:, c:c + 1],
                                          scale=gam_s[:, c:c + 1]).then_inc(s1, 1)
                        sc.wait_ge(t1, i * nT + 16)
                        sc.copy(xqT[:, j % NB, :, :].rearrange("p a b -> p (a b)"),
                                bass.AP(ps_q8, 0, [[2 * D, 128], [2, D]])).then_inc(s1, 1)
                    else:
                        _fill(sc, s1, 9, junk_s, jsn)
                    j = i - 4
                    if 0 <= j < NT:
                        sq = j // TPS
                        tr = (j % TPS) * 128
                        sc.wait_ge(t1, i * nT + 25)
                        sc.copy(d_tok[:, sq, W + tr:W + tr + 128],
                                ps_db[0:16, j % NB, :]).then_inc(s1, 1)
                        sc.copy(b_tok[:, sq, W + tr:W + tr + 128],
                                ps_db[32:48, j % NB, :]).then_inc(s1, 1)
                    else:
                        _fill(sc, s1, 2, junk_s, jsn)

            @blk.gpsimd
            def _(gp):
                for i in range(NT + 4):
                    j = i - 2
                    if 0 <= j < NT:
                        gp.wait_ge(v1, (i - 1) * nV + 7)
                        if j >= NB:
                            gp.wait_ge(t1, (i - 1) * nT + 16)
                        gp.tensor_scalar(xq[:, j % NB, N0:D], xns[:, j % NBI, N0:D], FP8MAX, -FP8MAX,
                                         ALU.min, ALU.max).then_inc(g1, 1)
                    else:
                        _fill(gp, g1, 1, junk_g, jgn)
                    j = i - 2
                    if 0 <= j < NT:
                        if j >= 3:
                            gp.wait_ge(v1, (i - 1) * nV + 13)
                        gp.tensor_add(s_tmp[:, j % 3, :], xs[:, j % NBX, :],
                                      xns[:, j % NBI, :]).then_inc(g1, 1)
                    else:
                        _fill(gp, g1, 1, junk_g, jgn)
                    if i == 0:
                        gp.dma_start(wg_hi[:, :, :],
                                     bass.AP(wg_d.tensor, 0, [[D, 128], [128 * D, KD], [1, D]])).then_inc(pdh, 16)
                        for c in range(2):
                            gp.dma_start(wgf_r[:, c, :],
                                         wg_d[128 * c:128 * (c + 1), :]).then_inc(pdw[c], 16)
                    c = i - WG0
                    if 0 <= c < KD:
                        if c == 0:
                            gp.wait_ge(pdh, 16)
                        gp.wait_ge(pdw[c % 2], 16 * (c // 2 + 1))
                        gp.tensor_tensor(wg_lo[:, c, :], wgf_r[:, c % 2, :],
                                         wg_hi[:, c, :], ALU.subtract).then_inc(pL, 1)
                        if c + 2 < KD:
                            gp.wait_ge(pL, c + 1)
                            gp.dma_start(wgf_r[:, c % 2, :],
                                         wg_d[128 * (c + 2):128 * (c + 3), :]).then_inc(pdw[c % 2], 16)

            @blk.tensor
            def _(t):
                for i in range(NT + 4):
                    j = i - 3
                    if 0 <= j < NT:
                        t.wait_ge(s1, (j + 2) * nS + 4)
                        if j >= NB:
                            t.wait_ge(s1, (i - 2) * nS + 12)
                        for c in range(KD):
                            tr = t.transpose(ps_sh[j % NB][:, 128 * c:128 * (c + 1)],
                                             xn2s[:, j % NBI, 128 * c:128 * (c + 1)],
                                             ident[:, :])
                            if c == KD - 1:
                                tr.then_inc(t1, 8)
                        t.wait_ge(s1, (i - 1) * nS + 3)
                        if j >= 1:
                            t.wait_ge(s1, (i - 1) * nS + 13)
                        for c in range(KD // 2):
                            t.transpose(bass.AP(ps_q8, 2 * 128 * c, [[2 * D, 128], [2, 128]]),
                                        xq[:, j % NB, 128 * c:128 * (c + 1)],
                                        ident8[:, :])
                        t.wait_ge(g1, (i - 1) * nG + 1)
                        for c in range(KD // 2, KD):
                            tr = t.transpose(bass.AP(ps_q8, 2 * 128 * c, [[2 * D, 128], [2, 128]]),
                                             xq[:, j % NB, 128 * c:128 * (c + 1)],
                                             ident8[:, :])
                            if c == KD - 1:
                                tr.then_inc(t1, 8)
                    else:
                        _fill(t, t1, 16, None, None)
                    j = i - 4
                    if 0 <= j < NT:
                        t.wait_ge(s1, (i - 1) * nS + 12)
                        if j >= 1:
                            t.wait_ge(s1, (i - 1) * nS + 15)
                        for c in range(KD):
                            t.matmul(ps_db[:, j % NB, :], wdb[:, c, :], x2T[:, j % 3, c, :],
                                     start=(c == 0), stop=False)
                        t.matmul(ps_db[:, j % NB, :], bdb[0:1, :], ones_r[0:1, :],
                                 start=False, stop=True).then_inc(t1, 9)
                        t.wait_ge(s1, (i - 1) * nS + 13)
                        if j == 0:
                            t.wait_ge(pdq, 16 * KD)
                        if j >= 1:
                            t.wait_ge(v1, (i - 1) * nV + 13)
                        for h in range(2):
                            for c in range(0, KD, 2):
                                t.matmul(ps_ffn[:, h * N0:(h + 1) * N0], xqT[:, j % NB, c:c + 2, :],
                                         wqT[:, c:c + 2, h * N0:(h + 1) * N0],
                                         start=(c == 0), stop=False,
                                         perf_mode=mybir.MatmulPerfMode.DoubleRow)
                            mm = t.matmul(ps_ffn[:, h * N0:(h + 1) * N0], ones2b[0:2, :],
                                          fb2[0:2, h * N0:(h + 1) * N0], start=False, stop=True)
                            if h == 1:
                                mm.then_inc(t1, 10)
                    else:
                        _fill(t, t1, 19, None, None)

        p1_st.close()

        p2_st = ExitStack()
        ps_sc = [p2_st.enter_context(nc.psum_tensor(f"ps_sc{i}", [S, LB], F32)) for i in range(4)]

        def ap_tok(tens, g, col, n, step):
            return bass.AP(tens, g * WT + col, [[n_seq * WT, S], [step, n]])

        with nc.Block() as blk:
            @blk.scalar
            def _(sc):
                sc.activation(d_tok[:, :, :], d_tok[:, :, :], AF.Sigmoid).then_inc(s2s, 1)
                for i in range(NSTEP):
                    for g in range(n_seq):
                        if i == 0:
                            sc.wait_ge(t2s, 1 + g)
                        else:
                            sc.wait_ge(t2s, 2 + 4 * (i - 1) + 2 + 2 * g)
                        sc.activation(ap_tok(h_tok, g, i, LB, L),
                                      ps_sc[2 * g + i % 2][:, :], AF.Tanh).then_inc(s2s, 1)

            @blk.vector
            def _(v):
                for i in range(1, NSTEP):
                    for g in range(n_seq):
                        v.wait_ge(s2s, 1 + 2 * (i - 1) + g + 1)
                        v.tensor_mul(hdb[:, i % 2, g, :],
                                     ap_tok(h_tok, g, i - 1, LB, L),
                                     ap_tok(d_tok, g, i, LB, L)).then_inc(v2s, 1)

            @blk.tensor
            def _(t):
                t.wait_ge(s2s, 1)
                t.matmul(ps_sc[0][:, :], ident[0:S, 0:S], ap_tok(b_tok, 0, 0, LB, L),
                         start=True, stop=True).then_inc(t2s, 1)
                t.matmul(ps_sc[2][:, :], ident[0:S, 0:S], ap_tok(b_tok, 1, 0, LB, L),
                         start=True, stop=True).then_inc(t2s, 1)
                for i in range(1, NSTEP):
                    for g in range(n_seq):
                        if i >= 2:
                            t.wait_ge(s2s, 1 + 2 * (i - 2) + g + 1)
                        t.matmul(ps_sc[2 * g + i % 2][:, :], ident[0:S, 0:S],
                                 ap_tok(b_tok, g, i, LB, L),
                                 start=True, stop=False).then_inc(t2s, 1)
                        t.wait_ge(v2s, 2 * (i - 1) + g + 1)
                        t.matmul(ps_sc[2 * g + i % 2][:, :], a16[:, :], hdb[:, i % 2, g, :],
                                 start=False, stop=True).then_inc(t2s, 1)

        p2_st.close()

        nV3, nS3, nT3, nG3 = 8, 2, 54, 1
        p3_st = ExitStack()
        t3hi = p3_st.enter_context(nc.sbuf_tensor("t3hi", [128, 3, KD, 128], BF16))
        t3lo = p3_st.enter_context(nc.sbuf_tensor("t3lo", [128, 3, KD, 128], BF16))
        gsb = p3_st.enter_context(nc.sbuf_tensor("gsb", [128, NB, D], F32))
        hs3 = p3_st.enter_context(nc.sbuf_tensor("hs3", [3 * S, NBP, 128], BF16))
        hs_tl = p3_st.enter_context(nc.sbuf_tensor("hs_tl", [S, NBP, 128], BF16))
        ps_sh3 = [p3_st.enter_context(nc.psum_tensor(f"ps_sh3{i}", [128, D], F32)) for i in range(3)]
        ps_g = [p3_st.enter_context(nc.psum_tensor(f"ps_g{i}", [128, N0], F32)) for i in range(2)]

        with nc.Block() as blk:
            @blk.sync
            def _(sync):
                for k in range(NT + 5):
                    j = k - 5
                    if 0 <= j < NT:
                        sync.wait_ge(g3, (j + 4) * nG3)
                        sync.dma_start(out_d[128 * j:128 * (j + 1), :], part[:, j % NB, :]).then_inc(so3[j % NB], 16)
                    if k < NT:
                        if k < 3:
                            sync.wait_ge(spx[k], 16 * ((NT - k + 2) // 3))
                        else:
                            sync.wait_ge(v3, (k - 1) * nV3 + 2)
                        sync.dma_start(x2T[:, k % 3, :, :],
                                       x2t_d[128 * k:128 * (k + 1), :]).then_inc(sxt[k % 3], 16)
                        if k >= NBP:
                            sync.wait_ge(g3, k * nG3)
                        sync.dma_start(xns[:, k % NBP, :], out_d[128 * k:128 * (k + 1), :]).then_inc(spi[k % NBP], 16)
                        sync.wait_ge(v3, k * nV3 + 8)
                        sync.dma_start(hs3[S:2 * S, k % NBP, :], hs3[0:S, k % NBP, :]).then_inc(sdh[k % NBP], 16)
                        sync.dma_start(hs3[2 * S:3 * S, k % NBP, :], hs_tl[:, k % NBP, :]).then_inc(sdh[k % NBP], 16)
                for j in range(NB):
                    sync.wait_ge(so3[j], 16 * ((NT - j + NB - 1) // NB))

            @blk.vector
            def _(v):
                for k in range(NT + 3):
                    j = k - 2
                    if 0 <= j < NT:
                        v.wait_ge(sxt[j % 3], 16 * (j // 3 + 1))
                        if j >= 3:
                            v.wait_ge(t3, (k - 2) * nT3 + 40)
                        v.tensor_copy(t3hi[:, j % 3, :, :], x2T[:, j % 3, :, :]).then_inc(v3, 1)
                        v.wait_ge(v3, k * nV3 + 1)
                        if j >= 3:
                            v.wait_ge(t3, (k - 2) * nT3 + 48)
                        v.tensor_tensor(t3lo[:, j % 3, :, :], x2T[:, j % 3, :, :], t3hi[:, j % 3, :, :],
                                        ALU.subtract).then_inc(v3, 1)
                    else:
                        _fill(v, v3, 2, junk_v, jvn)
                    j = k - 3
                    if 0 <= j < NT:
                        v.wait_ge(t3, k * nT3 + 24)
                        if j >= NB:
                            v.wait_ge(s3, (k - 2) * nS3 + 1)
                        v.tensor_add(gsb[:, j % NB, 0:N0], ps_g[0][:, :], bgate_bc[:, 0:N0]).then_inc(v3, 1)
                        v.wait_ge(t3, k * nT3 + 48)
                        if j >= NB:
                            v.wait_ge(s3, (k - 2) * nS3 + 2)
                        v.tensor_add(gsb[:, j % NB, N0:D], ps_g[1][:, :], bgate_bc[:, N0:D]).then_inc(v3, 1)
                        v.wait_ge(t3, k * nT3 + 54)
                        v.wait_ge(s3, k * nS3 + 1)
                        if k >= 4:
                            v.wait_ge(g3, k * nG3)
                        v.tensor_mul(s_tmp[:, j % NB, 0:N0], ps_sh3[j % 3][:, 0:N0],
                                     gsb[:, j % NB, 0:N0]).then_inc(v3, 1)
                        v.wait_ge(s3, k * nS3 + 2)
                        v.tensor_mul(s_tmp[:, j % NB, N0:D], ps_sh3[j % 3][:, N0:D],
                                     gsb[:, j % NB, N0:D]).then_inc(v3, 1)
                    else:
                        _fill(v, v3, 4, junk_v, jvn)
                    if k < NT:
                        sq3 = k // TPS
                        tr3 = (k % TPS) * 128
                        if k == 0:
                            v.wait_ge(s2s, 1 + 2 * NSTEP)
                        if k >= NBP:
                            v.wait_ge(t3, (k - 1) * nT3 + 54)
                        v.tensor_copy(hs3[0:S, k % NBP, :],
                                      h_tok[:, sq3, W + tr3:W + tr3 + 128]).then_inc(v3, 1)
                        v.wait_ge(v3, k * nV3 + 7)
                        v.tensor_tensor(hs_tl[:, k % NBP, :],
                                        h_tok[:, sq3, W + tr3:W + tr3 + 128],
                                        hs3[0:S, k % NBP, :], ALU.subtract).then_inc(v3, 1)
                    else:
                        _fill(v, v3, 2, junk_v, jvn)

            @blk.scalar
            def _(sc):
                for k in range(NT + 3):
                    j = k - 3
                    if 0 <= j < NT:
                        sc.wait_ge(v3, k * nV3 + 3)
                        sc.activation(gsb[:, j % NB, 0:N0], gsb[:, j % NB, 0:N0], AF.Sigmoid).then_inc(s3, 1)
                        sc.wait_ge(v3, k * nV3 + 4)
                        sc.activation(gsb[:, j % NB, N0:D], gsb[:, j % NB, N0:D], AF.Sigmoid).then_inc(s3, 1)
                    else:
                        _fill(sc, s3, 2, junk_s, jsn)

            @blk.gpsimd
            def _(gp):
                for k in range(NT + 3):
                    j = k - 3
                    if 0 <= j < NT:
                        gp.wait_ge(v3, k * nV3 + 6)
                        gp.wait_ge(spi[j % NBP], 16 * (j // NBP + 1))
                        if j >= NB:
                            gp.wait_ge(so3[j % NB], 16 * ((j - NB) // NB + 1))
                        gp.tensor_add(part[:, j % NB, :], s_tmp[:, j % NB, :],
                                      xns[:, j % NBP, :]).then_inc(g3, 1)
                    else:
                        _fill(gp, g3, 1, junk_g, jgn)

            @blk.tensor
            def _(t):
                for k in range(NT + 3):
                    j = k - 3
                    if 0 <= j < NT:
                        t.wait_ge(v3, (k - 1) * nV3 + 2)
                        if j == 0:
                            t.wait_ge(pL, KD)
                        for h in range(2):
                            if j >= 1:
                                t.wait_ge(v3, (k - 1) * nV3 + 3 + h)
                            for c in range(KD):
                                t.matmul(ps_g[h][:, :], t3hi[:, j % 3, c, :],
                                         wg_hi[:, c, h * N0:(h + 1) * N0],
                                         start=(c == 0), stop=False)
                                mm = t.matmul(ps_g[h][:, :], t3hi[:, j % 3, c, :],
                                              wg_lo[:, c, h * N0:(h + 1) * N0],
                                              start=False, stop=False)
                                if h == 1 and c == KD - 1:
                                    mm.then_inc(t3, 16)
                            for c in range(KD):
                                mm = t.matmul(ps_g[h][:, :], t3lo[:, j % 3, c, :],
                                              wg_hi[:, c, h * N0:(h + 1) * N0],
                                              start=False, stop=(c == KD - 1))
                                if c == KD - 1:
                                    mm.then_inc(t3, 24 if h == 0 else 8)
                        t.wait_ge(v3, (k - 3) * nV3 + 8)
                        t.wait_ge(sdh[j % NBP], 32 * (j // NBP + 1))
                        if j == 0:
                            t.wait_ge(pcm, 48)
                        for h in range(2):
                            t.matmul(ps_sh3[j % 3][:, h * N0:(h + 1) * N0], hs3[:, j % NBP, :],
                                     cm3[:, h * N0:(h + 1) * N0], start=True, stop=True)
                        t.ldweights(ones1b[0:1, 0:1]).then_inc(t3, 6)
                    else:
                        _fill(t, t3, 54, None, None)
        p3_st.close()

    return nc


def kernel(L0_attn_x, L0_attn_ln_gamma, L0_attn_ln_beta, L0_attn_A, L0_attn_C,
           L0_attn_w_delta, L0_attn_b_delta, L0_attn_w_in, L0_attn_b_in,
           L0_attn_w_gate, L0_attn_b_gate, L0_ffn_weight, L0_ffn_bias):
    B, T, _ = L0_attn_x.shape
    n_cores = 8
    spc = B // n_cores
    x = np.ascontiguousarray(np.asarray(L0_attn_x, dtype=np.float32))
    common = {
        "w_gate": np.ascontiguousarray(np.asarray(L0_attn_w_gate, np.float32)),
        "w_delta": np.ascontiguousarray(np.asarray(L0_attn_w_delta, np.float32)),
        "w_in": np.ascontiguousarray(np.asarray(L0_attn_w_in, np.float32)),
        "b_gate": np.asarray(L0_attn_b_gate, np.float32).reshape(1, -1),
        "b_delta": np.asarray(L0_attn_b_delta, np.float32).reshape(1, -1),
        "b_in": np.asarray(L0_attn_b_in, np.float32).reshape(1, -1),
        "ln_gamma": np.asarray(L0_attn_ln_gamma, np.float32).reshape(1, -1),
        "ln_beta": np.asarray(L0_attn_ln_beta, np.float32).reshape(1, -1),
        "a_mat": np.ascontiguousarray(np.asarray(L0_attn_A, np.float32)),
        "c_mat": np.ascontiguousarray(np.asarray(L0_attn_C, np.float32)),
        "ffn_wT": np.ascontiguousarray(np.asarray(L0_ffn_weight, np.float32).T),
        "ffn_b": np.asarray(L0_ffn_bias, np.float32).reshape(1, -1),
    }
    in_maps = []
    for c in range(n_cores):
        m = dict(common)
        m["x"] = np.ascontiguousarray(x[c * spc:(c + 1) * spc].reshape(spc * T, -1))
        in_maps.append(m)
    nc = build_nc(n_seq=spc, T=T)
    res = run_bass_kernel_spmd(nc, in_maps, core_ids=list(range(n_cores)))
    outs = [np.asarray(r["out"]).reshape(spc, T, -1) for r in res.results]
    return np.concatenate(outs, axis=0)


# version 63
# speedup vs baseline: 1.0295x; 1.0011x over previous
import numpy as np
from contextlib import ExitStack

import concourse.bass as bass
import concourse.mybir as mybir
from concourse.bass_utils import run_bass_kernel_spmd

F32 = mybir.dt.float32
BF16 = mybir.dt.bfloat16
FP8 = mybir.dt.float8e4
AF = mybir.ActivationFunctionType
ALU = mybir.AluOpType

D = 1024
KD = 8
S = 16
EPS = 1e-5
FP8MAX = 448.0
N0 = 512


def build_nc(n_seq=2, T=2048, C=128, W=16, debug=False, skip_p3_store=False):
    L = T // C
    assert W <= L and T % C == 0
    NSTEP = W + L
    LB = C
    WT = W + T
    NTOK = n_seq * T
    NT = NTOK // 128
    TPS = T // 128
    NB = 2
    NBI = 3

    nc = bass.Bass("TRN2", target_bir_lowering=False, debug=debug)

    x_d = nc.dram_tensor("x", [NTOK, D], F32, kind="ExternalInput").ap()
    wg_d = nc.dram_tensor("w_gate", [D, D], F32, kind="ExternalInput").ap()
    wdel_d = nc.dram_tensor("w_delta", [D, S], F32, kind="ExternalInput").ap()
    win_d = nc.dram_tensor("w_in", [D, S], F32, kind="ExternalInput").ap()
    bg_d = nc.dram_tensor("b_gate", [1, D], F32, kind="ExternalInput").ap()
    bdel_d = nc.dram_tensor("b_delta", [1, S], F32, kind="ExternalInput").ap()
    bin_d = nc.dram_tensor("b_in", [1, S], F32, kind="ExternalInput").ap()
    gam_d = nc.dram_tensor("ln_gamma", [1, D], F32, kind="ExternalInput").ap()
    bet_d = nc.dram_tensor("ln_beta", [1, D], F32, kind="ExternalInput").ap()
    amat_d = nc.dram_tensor("a_mat", [S, S], F32, kind="ExternalInput").ap()
    cmat_d = nc.dram_tensor("c_mat", [S, D], F32, kind="ExternalInput").ap()
    fwT_d = nc.dram_tensor("ffn_wT", [D, D], F32, kind="ExternalInput").ap()
    fb_d = nc.dram_tensor("ffn_b", [1, D], F32, kind="ExternalInput").ap()
    out_d = nc.dram_tensor("out", [NTOK, D], F32, kind="ExternalOutput").ap()
    x2t_d = nc.dram_tensor("x2t_scratch", [NTOK, D], F32).ap()

    st = ExitStack()
    sbuf = lambda n, sh, dt=F32: st.enter_context(nc.sbuf_tensor(n, sh, dt))
    sem = lambda n: st.enter_context(nc.semaphore(n))

    with st:
        wg_hi = sbuf("wg_hi", [128, KD, D], BF16)
        wg_lo = sbuf("wg_lo", [128, KD, D], BF16)
        wdb = sbuf("wdb", [128, KD, 64])
        bdb = sbuf("bdb", [1, 64])
        fb2 = sbuf("fb2", [2, D], BF16)
        ones2b = sbuf("ones2b", [2, 128], BF16)
        cm_hi = sbuf("cm_hi", [S, D], BF16)
        cm_lo = sbuf("cm_lo", [S, D], BF16)
        cm3 = sbuf("cm3", [3 * S, D], BF16)
        ones_r = sbuf("ones_r", [1, 128])
        ones1b = sbuf("ones1b", [1, 128], BF16)
        bgate_bc = sbuf("bgate_bc", [128, D])
        fb_hi = sbuf("fb_hi", [1, D], BF16)
        fb_lo = sbuf("fb_lo", [1, D], BF16)
        wqT = sbuf("wqT", [128, KD, D], FP8)
        a16 = sbuf("a16", [S, S])
        gam_s = sbuf("gam_s", [128, KD])
        bet_s = sbuf("bet_s", [128, KD])
        ident = sbuf("ident", [128, 128])
        ident8 = sbuf("ident8", [128, 128], FP8)
        eps_t = sbuf("eps_t", [128, 1])
        d_tok = sbuf("d_tok", [S, n_seq, WT])
        b_tok = sbuf("b_tok", [S, n_seq, WT])
        h_tok = sbuf("h_tok", [S, n_seq, WT])

        prep_st = ExitStack()
        iog = prep_st.enter_context(nc.sbuf_tensor("iog", [128, 128], mybir.dt.int32))
        iof = prep_st.enter_context(nc.sbuf_tensor("iof", [128, 128], mybir.dt.int32))
        bgf = prep_st.enter_context(nc.sbuf_tensor("bgf", [1, D], F32))
        cmat = prep_st.enter_context(nc.sbuf_tensor("cmat", [S, D], F32))
        fbf = prep_st.enter_context(nc.sbuf_tensor("fbf", [1, D], F32))

        pdq = sem("pdq"); pdh = sem("pdh"); pL = sem("pL"); pcm = sem("pcm")
        sdh = [sem(f"sdh{j}") for j in range(4)]
        pdw = [sem(f"pdw{j}") for j in range(2)]
        pd2a = sem("pd2a"); pd2b = sem("pd2b")
        pg = sem("pg"); pv = sem("pv"); pds = sem("pds"); pfb = sem("pfb")

        with nc.Block() as blk:
            pdn = [0]; pd2n = [0]

            @blk.sync
            def _(sync):
                for c in range(KD):
                    sync.dma_start(gam_s[:, c:c + 1], gam_d[0:1, 128 * c:128 * (c + 1)]).then_inc(pds, 16); pdn[0] += 16
                for c in range(KD):
                    sync.dma_start(bet_s[:, c:c + 1], bet_d[0:1, 128 * c:128 * (c + 1)]).then_inc(pds, 16); pdn[0] += 16
                sync.dma_start(a16[:, :], amat_d[:, :]).then_inc(pds, 16); pdn[0] += 16
                sync.dma_start(cmat[:, :], cmat_d[:, :]).then_inc(pds, 16); pdn[0] += 16

            @blk.scalar
            def _(sc):
                sc.dma_start(fbf[0:1, :], fb_d[0:1, :]).then_inc(pd2a, 16)
                sc.dma_start(bgate_bc[:, :], bass.AP(bg_d.tensor, 0, [[0, 128], [1, D]])).then_inc(pd2a, 16)
                sc.wait_ge(pg, 2)
                for c in range(KD):
                    sc.dma_start(wdb[:, c, 0:S], wdel_d[128 * c:128 * (c + 1), :]).then_inc(pd2b, 16); pd2n[0] += 16
                for c in range(KD):
                    sc.dma_start(wdb[:, c, 32:32 + S], win_d[128 * c:128 * (c + 1), :]).then_inc(pd2b, 16); pd2n[0] += 16
                sc.dma_start(bdb[0:1, 0:S], bdel_d[0:1, :]).then_inc(pd2b, 16); pd2n[0] += 16
                sc.dma_start(bdb[0:1, 32:32 + S], bin_d[0:1, :]).then_inc(pd2b, 16); pd2n[0] += 16
                sc.wait_ge(pv, 2)
                sc.dma_start(fb2[0:1, :], fb_hi[0:1, :]).then_inc(pfb, 16)
                sc.dma_start(fb2[1:2, :], fb_lo[0:1, :]).then_inc(pfb, 16)
                sc.wait_ge(pv, 6)
                sc.dma_start(cm3[0:S, :], cm_hi[:, :]).then_inc(pcm, 16)
                sc.dma_start(cm3[S:2 * S, :], cm_lo[:, :]).then_inc(pcm, 16)
                sc.dma_start(cm3[2 * S:3 * S, :], cm_hi[:, :]).then_inc(pcm, 16)

            @blk.gpsimd
            def _(gp):
                for c in range(KD):
                    gp.dma_start(wqT[:, c, :], fwT_d[128 * c:128 * (c + 1), :]).then_inc(pdq, 16)
                gp.memset(wdb[:, :, :], 0.0).then_inc(pg, 1)
                gp.memset(bdb[0:1, :], 0.0).then_inc(pg, 1)
                gp.memset(ones2b[:, :], 1.0)
                gp.memset(d_tok[:, :, 0:W], 0.0).then_inc(pg, 1)
                gp.memset(b_tok[:, :, 0:W], 0.0).then_inc(pg, 1)
                gp.memset(h_tok[:, :, 0:W], 0.0).then_inc(pg, 1)
                gp.memset(ones_r[0:1, :], 1.0).then_inc(pg, 1)
                gp.memset(ones1b[0:1, :], 1.0).then_inc(pg, 1)
                gp.iota(iog[:, :], [[0, 128]], channel_multiplier=1).then_inc(pg, 1)
                gp.iota(iof[:, :], [[1, 128]], channel_multiplier=0).then_inc(pg, 1)
                gp.memset(eps_t[:, :], EPS).then_inc(pg, 1)

            @blk.vector
            def _(v):
                v.wait_ge(pd2a, 32)
                v.tensor_copy(fb_hi[0:1, :], fbf[0:1, :]).then_inc(pv, 1)
                v.wait_ge(pv, 1)
                v.tensor_tensor(fb_lo[0:1, :], fbf[0:1, :], fb_hi[0:1, :], ALU.subtract).then_inc(pv, 1)
                v.wait_ge(pg, 9)
                v.tensor_tensor(ident[:, :], iog[:, :], iof[:, :], ALU.is_equal).then_inc(pv, 1)
                v.wait_ge(pv, 3)
                v.tensor_copy(ident8[:, :], ident[:, :]).then_inc(pv, 1)
                v.wait_ge(pds, pdn[0])
                v.tensor_copy(cm_hi[:, :], cmat[:, :]).then_inc(pv, 1)
                v.wait_ge(pv, 5)
                v.tensor_tensor(cm_lo[:, :], cmat[:, :], cm_hi[:, :], ALU.subtract).then_inc(pv, 1)
                v.wait_ge(pfb, 32)
                v.wait_ge(pd2b, pd2n[0])

        prep_st.close()

        NBX = 4
        NBP = 4
        xns = sbuf("xns", [128, NBP, D])
        xn2s = sbuf("xn2s", [128, NBI, D])
        x2T = sbuf("x2T", [128, 3, KD, 128])
        part = sbuf("part", [128, NB, D])
        s_tmp = sbuf("s_tmp", [128, 3, D])
        bn6a = sbuf("bn6a", [128, 12])
        nm2t = sbuf("nm2t", [128, 2])
        junk_v = sbuf("junk_v", [128, 16])
        junk_s = sbuf("junk_s", [128, 16])
        junk_g = sbuf("junk_g", [128, 16])
        ve1 = sbuf("ve1", [128, 1])
        ve2 = sbuf("ve2", [128, 1])
        rv1 = sbuf("rv1", [128, NT])
        rv2 = sbuf("rv2", [128, NT])
        mv1a = sbuf("mv1a", [128, NT, 2])
        mv2a = sbuf("mv2a", [128, NT, 2])
        r1a = sbuf("r1a", [128, NT])
        r2a = sbuf("r2a", [128, NT])
        hdb = sbuf("hdb", [S, 2, 2, LB])

        sx = [sem(f"sx{j}") for j in range(NBX)]
        spo = [sem(f"spo{j}") for j in range(NB)]
        v1 = sem("v1"); s1 = sem("s1"); t1 = sem("t1"); g1 = sem("g1")
        v2s = sem("v2s"); s2s = sem("s2s"); t2s = sem("t2s")
        spx = [sem(f"spx{j}") for j in range(3)]
        sxt = [sem(f"sxt{j}") for j in range(3)]
        spi = [sem(f"spi{j}") for j in range(NBP)]
        so3 = [sem(f"so3_{j}") for j in range(NB)]
        v3 = sem("v3"); s3 = sem("s3"); t3 = sem("t3"); g3 = sem("g3")

        jvn = [0]; jsn = [0]; jgn = [0]

        def _fill(eng, semh, n, junk, ctr):
            if junk is None:
                eng.ldweights(ones1b[0:1, 0:1]).then_inc(semh, n)
                return
            c = ctr[0] % 16; ctr[0] += 1
            if junk is junk_s:
                eng.copy(junk[:, c:c + 1], eps_t[:, :]).then_inc(semh, n)
            else:
                eng.tensor_copy(junk[:, c:c + 1], eps_t[:, :]).then_inc(semh, n)


        nV, nS, nT, nG = 13, 15, 35, 2
        WG0 = max(0, NT - 12)
        p1_st = ExitStack()
        xs = p1_st.enter_context(nc.sbuf_tensor("xs", [128, NBX, D], F32))
        wgf_r = p1_st.enter_context(nc.sbuf_tensor("wgf_r", [128, 2, D], F32))
        xq = p1_st.enter_context(nc.sbuf_tensor("xq", [128, NB, D], FP8))
        xqT = p1_st.enter_context(nc.sbuf_tensor("xqT", [128, NB, KD, 128], FP8))
        ps_sh = [p1_st.enter_context(nc.psum_tensor(f"ps_sh{i}", [128, D], F32)) for i in range(NB)]
        ps_db = p1_st.enter_context(nc.psum_tensor("ps_db", [64, NB, 128], F32))
        ps_q8 = p1_st.enter_context(nc.psum_tensor("ps_q8", [128, 2 * D], FP8))
        ps_ffn = p1_st.enter_context(nc.psum_tensor("ps_ffn", [128, D], F32))

        with nc.Block() as blk:
            @blk.sync
            def _(sync):
                for i in range(NT + 6):
                    j = i - 6
                    if 0 <= j < NT:
                        sync.wait_ge(v1, (j + 4) * nV + 13)
                        sync.dma_start(out_d[128 * j:128 * (j + 1), :], part[:, j % NB, :]).then_inc(spo[j % NB], 16)
                    j2 = i - 5
                    if 0 <= j2 < NT:
                        sync.wait_ge(s1, (i - 2) * nS + 12)
                        sync.dma_start(x2t_d[128 * j2:128 * (j2 + 1), :],
                                       x2T[:, j2 % 3, :, :]).then_inc(spx[j2 % 3], 16)
                    if i < NT:
                        if i >= NBX:
                            sync.wait_ge(g1, (i - 2) * nG + 2)
                            sync.wait_ge(v1, (i - 3) * nV + 7)
                        sync.dma_start(xs[:, i % NBX, :], x_d[128 * i:128 * (i + 1), :]).then_inc(sx[i % NBX], 16)
                for j in range(NB):
                    sync.wait_ge(spo[j], 16 * ((NT - j + NB - 1) // NB))

            @blk.vector
            def _(v):
                for i in range(NT + 4):
                    j = i - 2
                    if 0 <= j < NT:
                        v.wait_ge(s1, i * nS + 1)
                        v.tensor_scalar(nm2t[:, j % 2:j % 2 + 1], mv2a[:, j, 0:1], r2a[:, j:j + 1],
                                        -1.0, ALU.mult, ALU.mult).then_inc(v1, 1)
                    else:
                        _fill(v, v1, 1, junk_v, jvn)
                    if i < NT:
                        sl = i % NBX
                        v.wait_ge(sx[sl], 16 * (i // NBX + 1))
                        v.bn_stats(bn6a[:, 0:6], xs[:, sl, 0:N0]).then_inc(v1, 1)
                        v.bn_stats(bn6a[:, 6:12], xs[:, sl, N0:D]).then_inc(v1, 1)
                        v.wait_ge(v1, i * nV + 3)
                        v.bn_aggr(mv1a[:, i, :], bn6a[:, :]).then_inc(v1, 1)
                        v.wait_ge(v1, i * nV + 4)
                        v.tensor_scalar(ve1[:, :], mv1a[:, i, 1:2], EPS, None, ALU.add).then_inc(v1, 1)
                        v.wait_ge(v1, i * nV + 5)
                        v.reciprocal(rv1[:, i:i + 1], ve1[:, :]).then_inc(v1, 1)
                    else:
                        _fill(v, v1, 5, junk_v, jvn)
                    j = i - 1
                    if 0 <= j < NT:
                        sl = j % NBX
                        v.wait_ge(s1, i * nS + 2)
                        if j >= NBI:
                            v.wait_ge(g1, (i - 2) * nG + 2)
                            v.wait_ge(s1, (i - 2) * nS + 3)
                        v.tensor_scalar(xns[:, j % NBI, :], xs[:, sl, :], mv1a[:, j, 0:1], r1a[:, j:j + 1],
                                        ALU.subtract, ALU.mult).then_inc(v1, 1)
                        v.wait_ge(v1, i * nV + 7)
                        v.bn_stats(bn6a[:, 0:6], xns[:, j % NBI, 0:N0]).then_inc(v1, 1)
                        v.bn_stats(bn6a[:, 6:12], xns[:, j % NBI, N0:D]).then_inc(v1, 1)
                        v.wait_ge(v1, i * nV + 9)
                        v.bn_aggr(mv2a[:, j, :], bn6a[:, :]).then_inc(v1, 1)
                        v.wait_ge(v1, i * nV + 10)
                        v.tensor_scalar(ve2[:, :], mv2a[:, j, 1:2], EPS, None, ALU.add).then_inc(v1, 1)
                        v.wait_ge(v1, i * nV + 11)
                        v.reciprocal(rv2[:, j:j + 1], ve2[:, :]).then_inc(v1, 1)
                    else:
                        _fill(v, v1, 6, junk_v, jvn)
                    j = i - 4
                    if 0 <= j < NT:
                        v.wait_ge(t1, i * nT + 35)
                        v.wait_ge(g1, (i - 2) * nG + 2)
                        if j >= NB:
                            v.wait_ge(spo[j % NB], 16 * (j // NB))
                        v.tensor_add(part[:, j % NB, :], s_tmp[:, j % 3, :],
                                     ps_ffn[:, :]).then_inc(v1, 1)
                    else:
                        _fill(v, v1, 1, junk_v, jvn)

            @blk.scalar
            def _(sc):
                for i in range(NT + 4):
                    j = i - 2
                    if 0 <= j < NT:
                        sc.wait_ge(v1, (i - 1) * nV + 12)
                        sc.activation(r2a[:, j:j + 1], rv2[:, j:j + 1], AF.Sqrt).then_inc(s1, 1)
                    else:
                        _fill(sc, s1, 1, junk_s, jsn)
                    j = i - 1
                    if 0 <= j < NT:
                        sc.wait_ge(v1, (i - 1) * nV + 6)
                        sc.activation(r1a[:, j:j + 1], rv1[:, j:j + 1], AF.Sqrt).then_inc(s1, 1)
                    else:
                        _fill(sc, s1, 1, junk_s, jsn)
                    j = i - 2
                    if 0 <= j < NT:
                        sc.wait_ge(v1, (i - 1) * nV + 7)
                        if j >= NB:
                            sc.wait_ge(t1, (i - 1) * nT + 16)
                        sc.copy(xq[:, j % NB, 0:N0], xns[:, j % NBI, 0:N0]).then_inc(s1, 1)
                    else:
                        _fill(sc, s1, 1, junk_s, jsn)
                    j = i - 2
                    if 0 <= j < NT:
                        sc.wait_ge(v1, i * nV + 1)
                        if j >= NB:
                            sc.wait_ge(t1, (i - 2) * nT + 8)
                        sc.activation(xn2s[:, j % NBI, :], xns[:, j % NBI, :], AF.Identity,
                                      bias=nm2t[:, j % 2:j % 2 + 1],
                                      scale=r2a[:, j:j + 1]).then_inc(s1, 1)
                    else:
                        _fill(sc, s1, 1, junk_s, jsn)
                    j = i - 3
                    if 0 <= j < NT:
                        sc.wait_ge(t1, i * nT + 8)
                        if j >= 3:
                            sc.wait_ge(spx[j % 3], 16 * (j // 3))
                        for c in range(KD):
                            act = sc.activation(x2T[:, j % 3, c, :], ps_sh[j % NB][:, 128 * c:128 * (c + 1)],
                                                AF.Identity, bias=bet_s[:, c:c + 1],
                                                scale=gam_s[:, c:c + 1])
                            if c == KD - 1:
                                act.then_inc(s1, 8)
                        sc.wait_ge(t1, i * nT + 16)
                        sc.copy(xqT[:, j % NB, :, :].rearrange("p a b -> p (a b)"),
                                bass.AP(ps_q8, 0, [[2 * D, 128], [2, D]])).then_inc(s1, 1)
                    else:
                        _fill(sc, s1, 9, junk_s, jsn)
                    j = i - 4
                    if 0 <= j < NT:
                        sq = j // TPS
                        tr = (j % TPS) * 128
                        sc.wait_ge(t1, i * nT + 25)
                        sc.copy(d_tok[:, sq, W + tr:W + tr + 128],
                                ps_db[0:16, j % NB, :])
                        sc.copy(b_tok[:, sq, W + tr:W + tr + 128],
                                ps_db[32:48, j % NB, :]).then_inc(s1, 2)
                    else:
                        _fill(sc, s1, 2, junk_s, jsn)

            @blk.gpsimd
            def _(gp):
                for i in range(NT + 4):
                    j = i - 2
                    if 0 <= j < NT:
                        gp.wait_ge(v1, (i - 1) * nV + 7)
                        if j >= NB:
                            gp.wait_ge(t1, (i - 1) * nT + 16)
                        gp.tensor_scalar(xq[:, j % NB, N0:D], xns[:, j % NBI, N0:D], FP8MAX, -FP8MAX,
                                         ALU.min, ALU.max).then_inc(g1, 1)
                    else:
                        _fill(gp, g1, 1, junk_g, jgn)
                    j = i - 2
                    if 0 <= j < NT:
                        if j >= 3:
                            gp.wait_ge(v1, (i - 1) * nV + 13)
                        gp.tensor_add(s_tmp[:, j % 3, :], xs[:, j % NBX, :],
                                      xns[:, j % NBI, :]).then_inc(g1, 1)
                    else:
                        _fill(gp, g1, 1, junk_g, jgn)
                    if i == 0:
                        gp.dma_start(wg_hi[:, :, :],
                                     bass.AP(wg_d.tensor, 0, [[D, 128], [128 * D, KD], [1, D]])).then_inc(pdh, 16)
                        for c in range(2):
                            gp.dma_start(wgf_r[:, c, :],
                                         wg_d[128 * c:128 * (c + 1), :]).then_inc(pdw[c], 16)
                    c = i - WG0
                    if 0 <= c < KD:
                        if c == 0:
                            gp.wait_ge(pdh, 16)
                        gp.wait_ge(pdw[c % 2], 16 * (c // 2 + 1))
                        gp.tensor_tensor(wg_lo[:, c, :], wgf_r[:, c % 2, :],
                                         wg_hi[:, c, :], ALU.subtract).then_inc(pL, 1)
                        if c + 2 < KD:
                            gp.wait_ge(pL, c + 1)
                            gp.dma_start(wgf_r[:, c % 2, :],
                                         wg_d[128 * (c + 2):128 * (c + 3), :]).then_inc(pdw[c % 2], 16)

            @blk.tensor
            def _(t):
                for i in range(NT + 4):
                    j = i - 3
                    if 0 <= j < NT:
                        t.wait_ge(s1, (j + 2) * nS + 4)
                        if j >= NB:
                            t.wait_ge(s1, (i - 2) * nS + 12)
                        for c in range(KD):
                            tr = t.transpose(ps_sh[j % NB][:, 128 * c:128 * (c + 1)],
                                             xn2s[:, j % NBI, 128 * c:128 * (c + 1)],
                                             ident[:, :])
                            if c == KD - 1:
                                tr.then_inc(t1, 8)
                        t.wait_ge(s1, (i - 1) * nS + 3)
                        if j >= 1:
                            t.wait_ge(s1, (i - 1) * nS + 13)
                        for c in range(KD // 2):
                            t.transpose(bass.AP(ps_q8, 2 * 128 * c, [[2 * D, 128], [2, 128]]),
                                        xq[:, j % NB, 128 * c:128 * (c + 1)],
                                        ident8[:, :])
                        t.wait_ge(g1, (i - 1) * nG + 1)
                        for c in range(KD // 2, KD):
                            tr = t.transpose(bass.AP(ps_q8, 2 * 128 * c, [[2 * D, 128], [2, 128]]),
                                             xq[:, j % NB, 128 * c:128 * (c + 1)],
                                             ident8[:, :])
                            if c == KD - 1:
                                tr.then_inc(t1, 8)
                    else:
                        _fill(t, t1, 16, None, None)
                    j = i - 4
                    if 0 <= j < NT:
                        t.wait_ge(s1, (i - 1) * nS + 12)
                        if j >= 1:
                            t.wait_ge(s1, (i - 1) * nS + 15)
                        for c in range(KD):
                            t.matmul(ps_db[:, j % NB, :], wdb[:, c, :], x2T[:, j % 3, c, :],
                                     start=(c == 0), stop=False)
                        t.matmul(ps_db[:, j % NB, :], bdb[0:1, :], ones_r[0:1, :],
                                 start=False, stop=True).then_inc(t1, 9)
                        t.wait_ge(s1, (i - 1) * nS + 13)
                        if j == 0:
                            t.wait_ge(pdq, 16 * KD)
                        if j >= 1:
                            t.wait_ge(v1, (i - 1) * nV + 13)
                        for h in range(2):
                            for c in range(0, KD, 2):
                                t.matmul(ps_ffn[:, h * N0:(h + 1) * N0], xqT[:, j % NB, c:c + 2, :],
                                         wqT[:, c:c + 2, h * N0:(h + 1) * N0],
                                         start=(c == 0), stop=False,
                                         perf_mode=mybir.MatmulPerfMode.DoubleRow)
                            mm = t.matmul(ps_ffn[:, h * N0:(h + 1) * N0], ones2b[0:2, :],
                                          fb2[0:2, h * N0:(h + 1) * N0], start=False, stop=True)
                            if h == 1:
                                mm.then_inc(t1, 10)
                    else:
                        _fill(t, t1, 19, None, None)

        p1_st.close()

        p2_st = ExitStack()
        ps_sc = [p2_st.enter_context(nc.psum_tensor(f"ps_sc{i}", [S, LB], F32)) for i in range(4)]

        def ap_tok(tens, g, col, n, step):
            return bass.AP(tens, g * WT + col, [[n_seq * WT, S], [step, n]])

        with nc.Block() as blk:
            @blk.scalar
            def _(sc):
                sc.activation(d_tok[:, :, :], d_tok[:, :, :], AF.Sigmoid).then_inc(s2s, 1)
                for i in range(NSTEP):
                    for g in range(n_seq):
                        if i == 0:
                            sc.wait_ge(t2s, 1 + g)
                        else:
                            sc.wait_ge(t2s, 2 + 4 * (i - 1) + 2 + 2 * g)
                        sc.activation(ap_tok(h_tok, g, i, LB, L),
                                      ps_sc[2 * g + i % 2][:, :], AF.Tanh).then_inc(s2s, 1)

            @blk.vector
            def _(v):
                for i in range(1, NSTEP):
                    for g in range(n_seq):
                        v.wait_ge(s2s, 1 + 2 * (i - 1) + g + 1)
                        v.tensor_mul(hdb[:, i % 2, g, :],
                                     ap_tok(h_tok, g, i - 1, LB, L),
                                     ap_tok(d_tok, g, i, LB, L)).then_inc(v2s, 1)

            @blk.tensor
            def _(t):
                t.wait_ge(s2s, 1)
                t.matmul(ps_sc[0][:, :], ident[0:S, 0:S], ap_tok(b_tok, 0, 0, LB, L),
                         start=True, stop=True).then_inc(t2s, 1)
                t.matmul(ps_sc[2][:, :], ident[0:S, 0:S], ap_tok(b_tok, 1, 0, LB, L),
                         start=True, stop=True).then_inc(t2s, 1)
                for i in range(1, NSTEP):
                    for g in range(n_seq):
                        if i >= 2:
                            t.wait_ge(s2s, 1 + 2 * (i - 2) + g + 1)
                        t.matmul(ps_sc[2 * g + i % 2][:, :], ident[0:S, 0:S],
                                 ap_tok(b_tok, g, i, LB, L),
                                 start=True, stop=False).then_inc(t2s, 1)
                        t.wait_ge(v2s, 2 * (i - 1) + g + 1)
                        t.matmul(ps_sc[2 * g + i % 2][:, :], a16[:, :], hdb[:, i % 2, g, :],
                                 start=False, stop=True).then_inc(t2s, 1)

        p2_st.close()

        nV3, nS3, nT3, nG3 = 8, 2, 54, 1
        p3_st = ExitStack()
        t3hi = p3_st.enter_context(nc.sbuf_tensor("t3hi", [128, 3, KD, 128], BF16))
        t3lo = p3_st.enter_context(nc.sbuf_tensor("t3lo", [128, 3, KD, 128], BF16))
        gsb = p3_st.enter_context(nc.sbuf_tensor("gsb", [128, NB, D], F32))
        hs3 = p3_st.enter_context(nc.sbuf_tensor("hs3", [3 * S, NBP, 128], BF16))
        hs_tl = p3_st.enter_context(nc.sbuf_tensor("hs_tl", [S, NBP, 128], BF16))
        ps_sh3 = [p3_st.enter_context(nc.psum_tensor(f"ps_sh3{i}", [128, D], F32)) for i in range(3)]
        ps_g = [p3_st.enter_context(nc.psum_tensor(f"ps_g{i}", [128, N0], F32)) for i in range(2)]

        with nc.Block() as blk:
            @blk.sync
            def _(sync):
                for k in range(NT + 5):
                    j = k - 5
                    if 0 <= j < NT:
                        sync.wait_ge(g3, (j + 4) * nG3)
                        sync.dma_start(out_d[128 * j:128 * (j + 1), :], part[:, j % NB, :]).then_inc(so3[j % NB], 16)
                    if k < NT:
                        if k < 3:
                            sync.wait_ge(spx[k], 16 * ((NT - k + 2) // 3))
                        else:
                            sync.wait_ge(v3, (k - 1) * nV3 + 2)
                        sync.dma_start(x2T[:, k % 3, :, :],
                                       x2t_d[128 * k:128 * (k + 1), :]).then_inc(sxt[k % 3], 16)
                        if k >= NBP:
                            sync.wait_ge(g3, k * nG3)
                        sync.dma_start(xns[:, k % NBP, :], out_d[128 * k:128 * (k + 1), :]).then_inc(spi[k % NBP], 16)
                        sync.wait_ge(v3, k * nV3 + 8)
                        sync.dma_start(hs3[S:2 * S, k % NBP, :], hs3[0:S, k % NBP, :]).then_inc(sdh[k % NBP], 16)
                        sync.dma_start(hs3[2 * S:3 * S, k % NBP, :], hs_tl[:, k % NBP, :]).then_inc(sdh[k % NBP], 16)
                for j in range(NB):
                    sync.wait_ge(so3[j], 16 * ((NT - j + NB - 1) // NB))

            @blk.vector
            def _(v):
                for k in range(NT + 3):
                    j = k - 2
                    if 0 <= j < NT:
                        v.wait_ge(sxt[j % 3], 16 * (j // 3 + 1))
                        if j >= 3:
                            v.wait_ge(t3, (k - 2) * nT3 + 40)
                        v.tensor_copy(t3hi[:, j % 3, :, :], x2T[:, j % 3, :, :]).then_inc(v3, 1)
                        v.wait_ge(v3, k * nV3 + 1)
                        if j >= 3:
                            v.wait_ge(t3, (k - 2) * nT3 + 48)
                        v.tensor_tensor(t3lo[:, j % 3, :, :], x2T[:, j % 3, :, :], t3hi[:, j % 3, :, :],
                                        ALU.subtract).then_inc(v3, 1)
                    else:
                        _fill(v, v3, 2, junk_v, jvn)
                    j = k - 3
                    if 0 <= j < NT:
                        v.wait_ge(t3, k * nT3 + 24)
                        if j >= NB:
                            v.wait_ge(s3, (k - 2) * nS3 + 1)
                        v.tensor_add(gsb[:, j % NB, 0:N0], ps_g[0][:, :], bgate_bc[:, 0:N0]).then_inc(v3, 1)
                        v.wait_ge(t3, k * nT3 + 48)
                        if j >= NB:
                            v.wait_ge(s3, (k - 2) * nS3 + 2)
                        v.tensor_add(gsb[:, j % NB, N0:D], ps_g[1][:, :], bgate_bc[:, N0:D]).then_inc(v3, 1)
                        v.wait_ge(t3, k * nT3 + 54)
                        v.wait_ge(s3, k * nS3 + 1)
                        if k >= 4:
                            v.wait_ge(g3, k * nG3)
                        v.tensor_mul(s_tmp[:, j % NB, 0:N0], ps_sh3[j % 3][:, 0:N0],
                                     gsb[:, j % NB, 0:N0]).then_inc(v3, 1)
                        v.wait_ge(s3, k * nS3 + 2)
                        v.tensor_mul(s_tmp[:, j % NB, N0:D], ps_sh3[j % 3][:, N0:D],
                                     gsb[:, j % NB, N0:D]).then_inc(v3, 1)
                    else:
                        _fill(v, v3, 4, junk_v, jvn)
                    if k < NT:
                        sq3 = k // TPS
                        tr3 = (k % TPS) * 128
                        if k == 0:
                            v.wait_ge(s2s, 1 + 2 * NSTEP)
                        if k >= NBP:
                            v.wait_ge(t3, (k - 1) * nT3 + 54)
                        v.tensor_copy(hs3[0:S, k % NBP, :],
                                      h_tok[:, sq3, W + tr3:W + tr3 + 128]).then_inc(v3, 1)
                        v.wait_ge(v3, k * nV3 + 7)
                        v.tensor_tensor(hs_tl[:, k % NBP, :],
                                        h_tok[:, sq3, W + tr3:W + tr3 + 128],
                                        hs3[0:S, k % NBP, :], ALU.subtract).then_inc(v3, 1)
                    else:
                        _fill(v, v3, 2, junk_v, jvn)

            @blk.scalar
            def _(sc):
                for k in range(NT + 3):
                    j = k - 3
                    if 0 <= j < NT:
                        sc.wait_ge(v3, k * nV3 + 3)
                        sc.activation(gsb[:, j % NB, 0:N0], gsb[:, j % NB, 0:N0], AF.Sigmoid).then_inc(s3, 1)
                        sc.wait_ge(v3, k * nV3 + 4)
                        sc.activation(gsb[:, j % NB, N0:D], gsb[:, j % NB, N0:D], AF.Sigmoid).then_inc(s3, 1)
                    else:
                        _fill(sc, s3, 2, junk_s, jsn)

            @blk.gpsimd
            def _(gp):
                for k in range(NT + 3):
                    j = k - 3
                    if 0 <= j < NT:
                        gp.wait_ge(v3, k * nV3 + 6)
                        gp.wait_ge(spi[j % NBP], 16 * (j // NBP + 1))
                        if j >= NB:
                            gp.wait_ge(so3[j % NB], 16 * ((j - NB) // NB + 1))
                        gp.tensor_add(part[:, j % NB, :], s_tmp[:, j % NB, :],
                                      xns[:, j % NBP, :]).then_inc(g3, 1)
                    else:
                        _fill(gp, g3, 1, junk_g, jgn)

            @blk.tensor
            def _(t):
                for k in range(NT + 3):
                    j = k - 3
                    if 0 <= j < NT:
                        t.wait_ge(v3, (k - 1) * nV3 + 2)
                        if j == 0:
                            t.wait_ge(pL, KD)
                        for h in range(2):
                            if j >= 1:
                                t.wait_ge(v3, (k - 1) * nV3 + 3 + h)
                            for c in range(KD):
                                t.matmul(ps_g[h][:, :], t3hi[:, j % 3, c, :],
                                         wg_hi[:, c, h * N0:(h + 1) * N0],
                                         start=(c == 0), stop=False)
                                mm = t.matmul(ps_g[h][:, :], t3hi[:, j % 3, c, :],
                                              wg_lo[:, c, h * N0:(h + 1) * N0],
                                              start=False, stop=False)
                                if h == 1 and c == KD - 1:
                                    mm.then_inc(t3, 16)
                            for c in range(KD):
                                mm = t.matmul(ps_g[h][:, :], t3lo[:, j % 3, c, :],
                                              wg_hi[:, c, h * N0:(h + 1) * N0],
                                              start=False, stop=(c == KD - 1))
                                if c == KD - 1:
                                    mm.then_inc(t3, 24 if h == 0 else 8)
                        t.wait_ge(v3, (k - 3) * nV3 + 8)
                        t.wait_ge(sdh[j % NBP], 32 * (j // NBP + 1))
                        if j == 0:
                            t.wait_ge(pcm, 48)
                        for h in range(2):
                            t.matmul(ps_sh3[j % 3][:, h * N0:(h + 1) * N0], hs3[:, j % NBP, :],
                                     cm3[:, h * N0:(h + 1) * N0], start=True, stop=True)
                        t.ldweights(ones1b[0:1, 0:1]).then_inc(t3, 6)
                    else:
                        _fill(t, t3, 54, None, None)
        p3_st.close()

    return nc


def kernel(L0_attn_x, L0_attn_ln_gamma, L0_attn_ln_beta, L0_attn_A, L0_attn_C,
           L0_attn_w_delta, L0_attn_b_delta, L0_attn_w_in, L0_attn_b_in,
           L0_attn_w_gate, L0_attn_b_gate, L0_ffn_weight, L0_ffn_bias):
    B, T, _ = L0_attn_x.shape
    n_cores = 8
    spc = B // n_cores
    x = np.ascontiguousarray(np.asarray(L0_attn_x, dtype=np.float32))
    common = {
        "w_gate": np.ascontiguousarray(np.asarray(L0_attn_w_gate, np.float32)),
        "w_delta": np.ascontiguousarray(np.asarray(L0_attn_w_delta, np.float32)),
        "w_in": np.ascontiguousarray(np.asarray(L0_attn_w_in, np.float32)),
        "b_gate": np.asarray(L0_attn_b_gate, np.float32).reshape(1, -1),
        "b_delta": np.asarray(L0_attn_b_delta, np.float32).reshape(1, -1),
        "b_in": np.asarray(L0_attn_b_in, np.float32).reshape(1, -1),
        "ln_gamma": np.asarray(L0_attn_ln_gamma, np.float32).reshape(1, -1),
        "ln_beta": np.asarray(L0_attn_ln_beta, np.float32).reshape(1, -1),
        "a_mat": np.ascontiguousarray(np.asarray(L0_attn_A, np.float32)),
        "c_mat": np.ascontiguousarray(np.asarray(L0_attn_C, np.float32)),
        "ffn_wT": np.ascontiguousarray(np.asarray(L0_ffn_weight, np.float32).T),
        "ffn_b": np.asarray(L0_ffn_bias, np.float32).reshape(1, -1),
    }
    in_maps = []
    for c in range(n_cores):
        m = dict(common)
        m["x"] = np.ascontiguousarray(x[c * spc:(c + 1) * spc].reshape(spc * T, -1))
        in_maps.append(m)
    nc = build_nc(n_seq=spc, T=T)
    res = run_bass_kernel_spmd(nc, in_maps, core_ids=list(range(n_cores)))
    outs = [np.asarray(r["out"]).reshape(spc, T, -1) for r in res.results]
    return np.concatenate(outs, axis=0)


# version 64
# speedup vs baseline: 1.0452x; 1.0153x over previous
import numpy as np
from contextlib import ExitStack

import concourse.bass as bass
import concourse.mybir as mybir
from concourse.bass_utils import run_bass_kernel_spmd

F32 = mybir.dt.float32
BF16 = mybir.dt.bfloat16
FP8 = mybir.dt.float8e4
AF = mybir.ActivationFunctionType
ALU = mybir.AluOpType

D = 1024
KD = 8
S = 16
EPS = 1e-5
FP8MAX = 448.0
N0 = 512


def build_nc(n_seq=2, T=2048, C=128, W=16, debug=False, skip_p3_store=False):
    L = T // C
    assert W <= L and T % C == 0
    NSTEP = W + L
    LB = C
    WT = W + T
    NTOK = n_seq * T
    NT = NTOK // 128
    TPS = T // 128
    NB = 2
    NBI = 3

    nc = bass.Bass("TRN2", target_bir_lowering=False, debug=debug)

    x_d = nc.dram_tensor("x", [NTOK, D], F32, kind="ExternalInput").ap()
    wg_d = nc.dram_tensor("w_gate", [D, D], F32, kind="ExternalInput").ap()
    wdel_d = nc.dram_tensor("w_delta", [D, S], F32, kind="ExternalInput").ap()
    win_d = nc.dram_tensor("w_in", [D, S], F32, kind="ExternalInput").ap()
    bg_d = nc.dram_tensor("b_gate", [1, D], F32, kind="ExternalInput").ap()
    bdel_d = nc.dram_tensor("b_delta", [1, S], F32, kind="ExternalInput").ap()
    bin_d = nc.dram_tensor("b_in", [1, S], F32, kind="ExternalInput").ap()
    gam_d = nc.dram_tensor("ln_gamma", [1, D], F32, kind="ExternalInput").ap()
    bet_d = nc.dram_tensor("ln_beta", [1, D], F32, kind="ExternalInput").ap()
    amat_d = nc.dram_tensor("a_mat", [S, S], F32, kind="ExternalInput").ap()
    cmat_d = nc.dram_tensor("c_mat", [S, D], F32, kind="ExternalInput").ap()
    fwT_d = nc.dram_tensor("ffn_wT", [D, D], F32, kind="ExternalInput").ap()
    fb_d = nc.dram_tensor("ffn_b", [1, D], F32, kind="ExternalInput").ap()
    out_d = nc.dram_tensor("out", [NTOK, D], F32, kind="ExternalOutput").ap()
    x2t_d = nc.dram_tensor("x2t_scratch", [NTOK, D], F32).ap()

    st = ExitStack()
    sbuf = lambda n, sh, dt=F32: st.enter_context(nc.sbuf_tensor(n, sh, dt))
    sem = lambda n: st.enter_context(nc.semaphore(n))

    with st:
        wg_hi = sbuf("wg_hi", [128, KD, D], BF16)
        wg_lo = sbuf("wg_lo", [128, KD, D], BF16)
        wdb = sbuf("wdb", [128, KD, 64])
        bdb = sbuf("bdb", [1, 64])
        fb2 = sbuf("fb2", [2, D], BF16)
        ones2b = sbuf("ones2b", [2, 128], BF16)
        cm_hi = sbuf("cm_hi", [S, D], BF16)
        cm_lo = sbuf("cm_lo", [S, D], BF16)
        cm3 = sbuf("cm3", [3 * S, D], BF16)
        ones_r = sbuf("ones_r", [1, 128])
        ones1b = sbuf("ones1b", [1, 128], BF16)
        bgate_bc = sbuf("bgate_bc", [128, D])
        fb_hi = sbuf("fb_hi", [1, D], BF16)
        fb_lo = sbuf("fb_lo", [1, D], BF16)
        wqT = sbuf("wqT", [128, KD, D], FP8)
        a16 = sbuf("a16", [S, S])
        gam_s = sbuf("gam_s", [128, KD])
        bet_s = sbuf("bet_s", [128, KD])
        ident = sbuf("ident", [128, 128])
        ident8 = sbuf("ident8", [128, 128], FP8)
        eps_t = sbuf("eps_t", [128, 1])
        d_tok = sbuf("d_tok", [S, n_seq, WT])
        b_tok = sbuf("b_tok", [S, n_seq, WT])
        h_tok = sbuf("h_tok", [S, n_seq, WT])

        prep_st = ExitStack()
        iog = prep_st.enter_context(nc.sbuf_tensor("iog", [128, 128], mybir.dt.int32))
        iof = prep_st.enter_context(nc.sbuf_tensor("iof", [128, 128], mybir.dt.int32))
        bgf = prep_st.enter_context(nc.sbuf_tensor("bgf", [1, D], F32))
        cmat = prep_st.enter_context(nc.sbuf_tensor("cmat", [S, D], F32))
        fbf = prep_st.enter_context(nc.sbuf_tensor("fbf", [1, D], F32))

        pdq = sem("pdq"); pdh = sem("pdh"); pL = sem("pL"); pcm = sem("pcm")
        sdh = [sem(f"sdh{j}") for j in range(4)]
        pdw = [sem(f"pdw{j}") for j in range(2)]
        pd2a = sem("pd2a"); pd2b = sem("pd2b")
        pg = sem("pg"); pv = sem("pv"); pds = sem("pds"); pfb = sem("pfb")

        with nc.Block() as blk:
            pdn = [0]; pd2n = [0]

            @blk.sync
            def _(sync):
                for c in range(KD):
                    sync.dma_start(gam_s[:, c:c + 1], gam_d[0:1, 128 * c:128 * (c + 1)]).then_inc(pds, 16); pdn[0] += 16
                for c in range(KD):
                    sync.dma_start(bet_s[:, c:c + 1], bet_d[0:1, 128 * c:128 * (c + 1)]).then_inc(pds, 16); pdn[0] += 16
                sync.dma_start(a16[:, :], amat_d[:, :]).then_inc(pds, 16); pdn[0] += 16
                sync.dma_start(cmat[:, :], cmat_d[:, :]).then_inc(pds, 16); pdn[0] += 16

            @blk.scalar
            def _(sc):
                sc.dma_start(fbf[0:1, :], fb_d[0:1, :]).then_inc(pd2a, 16)
                sc.dma_start(bgate_bc[:, :], bass.AP(bg_d.tensor, 0, [[0, 128], [1, D]])).then_inc(pd2a, 16)
                sc.wait_ge(pg, 2)
                for c in range(KD):
                    sc.dma_start(wdb[:, c, 0:S], wdel_d[128 * c:128 * (c + 1), :]).then_inc(pd2b, 16); pd2n[0] += 16
                for c in range(KD):
                    sc.dma_start(wdb[:, c, 32:32 + S], win_d[128 * c:128 * (c + 1), :]).then_inc(pd2b, 16); pd2n[0] += 16
                sc.dma_start(bdb[0:1, 0:S], bdel_d[0:1, :]).then_inc(pd2b, 16); pd2n[0] += 16
                sc.dma_start(bdb[0:1, 32:32 + S], bin_d[0:1, :]).then_inc(pd2b, 16); pd2n[0] += 16
                sc.wait_ge(pv, 2)
                sc.dma_start(fb2[0:1, :], fb_hi[0:1, :]).then_inc(pfb, 16)
                sc.dma_start(fb2[1:2, :], fb_lo[0:1, :]).then_inc(pfb, 16)
                sc.wait_ge(pv, 6)
                sc.dma_start(cm3[0:S, :], cm_hi[:, :]).then_inc(pcm, 16)
                sc.dma_start(cm3[S:2 * S, :], cm_lo[:, :]).then_inc(pcm, 16)
                sc.dma_start(cm3[2 * S:3 * S, :], cm_hi[:, :]).then_inc(pcm, 16)

            @blk.gpsimd
            def _(gp):
                for c in range(KD):
                    gp.dma_start(wqT[:, c, :], fwT_d[128 * c:128 * (c + 1), :]).then_inc(pdq, 16)
                gp.memset(wdb[:, :, :], 0.0).then_inc(pg, 1)
                gp.memset(bdb[0:1, :], 0.0).then_inc(pg, 1)
                gp.memset(ones2b[:, :], 1.0)
                gp.memset(d_tok[:, :, 0:W], 0.0).then_inc(pg, 1)
                gp.memset(b_tok[:, :, 0:W], 0.0).then_inc(pg, 1)
                gp.memset(h_tok[:, :, 0:W], 0.0).then_inc(pg, 1)
                gp.memset(ones_r[0:1, :], 1.0).then_inc(pg, 1)
                gp.memset(ones1b[0:1, :], 1.0).then_inc(pg, 1)
                gp.iota(iog[:, :], [[0, 128]], channel_multiplier=1).then_inc(pg, 1)
                gp.iota(iof[:, :], [[1, 128]], channel_multiplier=0).then_inc(pg, 1)
                gp.memset(eps_t[:, :], EPS).then_inc(pg, 1)

            @blk.vector
            def _(v):
                v.wait_ge(pd2a, 32)
                v.tensor_copy(fb_hi[0:1, :], fbf[0:1, :]).then_inc(pv, 1)
                v.wait_ge(pv, 1)
                v.tensor_tensor(fb_lo[0:1, :], fbf[0:1, :], fb_hi[0:1, :], ALU.subtract).then_inc(pv, 1)
                v.wait_ge(pg, 9)
                v.tensor_tensor(ident[:, :], iog[:, :], iof[:, :], ALU.is_equal).then_inc(pv, 1)
                v.wait_ge(pv, 3)
                v.tensor_copy(ident8[:, :], ident[:, :]).then_inc(pv, 1)
                v.wait_ge(pds, pdn[0])
                v.tensor_copy(cm_hi[:, :], cmat[:, :]).then_inc(pv, 1)
                v.wait_ge(pv, 5)
                v.tensor_tensor(cm_lo[:, :], cmat[:, :], cm_hi[:, :], ALU.subtract).then_inc(pv, 1)
                v.wait_ge(pfb, 32)
                v.wait_ge(pd2b, pd2n[0])

        prep_st.close()

        NBX = 4
        NBP = 4
        xns = sbuf("xns", [128, NBP, D])
        xn2s = sbuf("xn2s", [128, NBI, D])
        x2T = sbuf("x2T", [128, 3, KD, 128])
        part = sbuf("part", [128, NB, D])
        s_tmp = sbuf("s_tmp", [128, 3, D])
        bn6a = sbuf("bn6a", [128, 12])
        nm2t = sbuf("nm2t", [128, 2])
        junk_v = sbuf("junk_v", [128, 16])
        junk_s = sbuf("junk_s", [128, 16])
        junk_g = sbuf("junk_g", [128, 16])
        ve1 = sbuf("ve1", [128, 1])
        ve2 = sbuf("ve2", [128, 1])
        rv1 = sbuf("rv1", [128, NT])
        rv2 = sbuf("rv2", [128, NT])
        mv1a = sbuf("mv1a", [128, NT, 2])
        mv2a = sbuf("mv2a", [128, NT, 2])
        r1a = sbuf("r1a", [128, NT])
        r2a = sbuf("r2a", [128, NT])
        hdb = sbuf("hdb", [S, 2, 2, LB])

        sx = [sem(f"sx{j}") for j in range(NBX)]
        spo = [sem(f"spo{j}") for j in range(NB)]
        v1 = sem("v1"); s1 = sem("s1"); t1 = sem("t1"); g1 = sem("g1")
        v2s = sem("v2s"); s2s = sem("s2s"); t2s = sem("t2s")
        spx = [sem(f"spx{j}") for j in range(3)]
        sxt = [sem(f"sxt{j}") for j in range(3)]
        spi = [sem(f"spi{j}") for j in range(NBP)]
        so3 = [sem(f"so3_{j}") for j in range(NB)]
        v3 = sem("v3"); s3 = sem("s3"); t3 = sem("t3"); g3 = sem("g3")

        jvn = [0]; jsn = [0]; jgn = [0]

        def _fill(eng, semh, n, junk, ctr):
            if junk is None:
                eng.ldweights(ones1b[0:1, 0:1]).then_inc(semh, n)
                return
            c = ctr[0] % 16; ctr[0] += 1
            if junk is junk_s:
                eng.copy(junk[:, c:c + 1], eps_t[:, :]).then_inc(semh, n)
            else:
                eng.tensor_copy(junk[:, c:c + 1], eps_t[:, :]).then_inc(semh, n)


        nV, nS, nT, nG = 13, 15, 35, 2
        WG0 = max(0, NT - 12)
        p1_st = ExitStack()
        xs = p1_st.enter_context(nc.sbuf_tensor("xs", [128, NBX, D], F32))
        wgf_r = p1_st.enter_context(nc.sbuf_tensor("wgf_r", [128, 2, D], F32))
        xq = p1_st.enter_context(nc.sbuf_tensor("xq", [128, NB, D], FP8))
        xqT = p1_st.enter_context(nc.sbuf_tensor("xqT", [128, NB, KD, 128], FP8))
        ps_sh = [p1_st.enter_context(nc.psum_tensor(f"ps_sh{i}", [128, D], F32)) for i in range(NB)]
        ps_db = p1_st.enter_context(nc.psum_tensor("ps_db", [64, NB, 128], F32))
        ps_q8 = p1_st.enter_context(nc.psum_tensor("ps_q8", [128, 2 * D], FP8))
        ps_ffn = p1_st.enter_context(nc.psum_tensor("ps_ffn", [128, D], F32))

        with nc.Block() as blk:
            @blk.sync
            def _(sync):
                for i in range(NT + 6):
                    j = i - 6
                    if 0 <= j < NT:
                        sync.wait_ge(v1, (j + 4) * nV + 13)
                        sync.dma_start(out_d[128 * j:128 * (j + 1), :], part[:, j % NB, :]).then_inc(spo[j % NB], 16)
                    if i < NT:
                        if i >= NBX:
                            sync.wait_ge(g1, (i - 2) * nG + 2)
                            sync.wait_ge(v1, (i - 3) * nV + 7)
                        sync.dma_start(xs[:, i % NBX, :], x_d[128 * i:128 * (i + 1), :]).then_inc(sx[i % NBX], 16)
                for j in range(NB):
                    sync.wait_ge(spo[j], 16 * ((NT - j + NB - 1) // NB))

            @blk.vector
            def _(v):
                for i in range(NT + 4):
                    j = i - 2
                    if 0 <= j < NT:
                        v.wait_ge(s1, i * nS + 1)
                        v.tensor_scalar(nm2t[:, j % 2:j % 2 + 1], mv2a[:, j, 0:1], r2a[:, j:j + 1],
                                        -1.0, ALU.mult, ALU.mult).then_inc(v1, 1)
                    else:
                        _fill(v, v1, 1, junk_v, jvn)
                    if i < NT:
                        sl = i % NBX
                        v.wait_ge(sx[sl], 16 * (i // NBX + 1))
                        v.bn_stats(bn6a[:, 0:6], xs[:, sl, 0:N0]).then_inc(v1, 1)
                        v.bn_stats(bn6a[:, 6:12], xs[:, sl, N0:D]).then_inc(v1, 1)
                        v.wait_ge(v1, i * nV + 3)
                        v.bn_aggr(mv1a[:, i, :], bn6a[:, :]).then_inc(v1, 1)
                        v.wait_ge(v1, i * nV + 4)
                        v.tensor_scalar(ve1[:, :], mv1a[:, i, 1:2], EPS, None, ALU.add).then_inc(v1, 1)
                        v.wait_ge(v1, i * nV + 5)
                        v.reciprocal(rv1[:, i:i + 1], ve1[:, :]).then_inc(v1, 1)
                    else:
                        _fill(v, v1, 5, junk_v, jvn)
                    j = i - 1
                    if 0 <= j < NT:
                        sl = j % NBX
                        v.wait_ge(s1, i * nS + 2)
                        if j >= NBI:
                            v.wait_ge(g1, (i - 2) * nG + 2)
                            v.wait_ge(s1, (i - 2) * nS + 3)
                        v.tensor_scalar(xns[:, j % NBI, :], xs[:, sl, :], mv1a[:, j, 0:1], r1a[:, j:j + 1],
                                        ALU.subtract, ALU.mult).then_inc(v1, 1)
                        v.wait_ge(v1, i * nV + 7)
                        v.bn_stats(bn6a[:, 0:6], xns[:, j % NBI, 0:N0]).then_inc(v1, 1)
                        v.bn_stats(bn6a[:, 6:12], xns[:, j % NBI, N0:D]).then_inc(v1, 1)
                        v.wait_ge(v1, i * nV + 9)
                        v.bn_aggr(mv2a[:, j, :], bn6a[:, :]).then_inc(v1, 1)
                        v.wait_ge(v1, i * nV + 10)
                        v.tensor_scalar(ve2[:, :], mv2a[:, j, 1:2], EPS, None, ALU.add).then_inc(v1, 1)
                        v.wait_ge(v1, i * nV + 11)
                        v.reciprocal(rv2[:, j:j + 1], ve2[:, :]).then_inc(v1, 1)
                    else:
                        _fill(v, v1, 6, junk_v, jvn)
                    j = i - 4
                    if 0 <= j < NT:
                        v.wait_ge(t1, i * nT + 35)
                        v.wait_ge(g1, (i - 2) * nG + 2)
                        if j >= NB:
                            v.wait_ge(spo[j % NB], 16 * (j // NB))
                        v.tensor_add(part[:, j % NB, :], s_tmp[:, j % 3, :],
                                     ps_ffn[:, :]).then_inc(v1, 1)
                    else:
                        _fill(v, v1, 1, junk_v, jvn)

            @blk.scalar
            def _(sc):
                for i in range(NT + 4):
                    j = i - 2
                    if 0 <= j < NT:
                        sc.wait_ge(v1, (i - 1) * nV + 12)
                        sc.activation(r2a[:, j:j + 1], rv2[:, j:j + 1], AF.Sqrt).then_inc(s1, 1)
                    else:
                        _fill(sc, s1, 1, junk_s, jsn)
                    j = i - 1
                    if 0 <= j < NT:
                        sc.wait_ge(v1, (i - 1) * nV + 6)
                        sc.activation(r1a[:, j:j + 1], rv1[:, j:j + 1], AF.Sqrt).then_inc(s1, 1)
                    else:
                        _fill(sc, s1, 1, junk_s, jsn)
                    j = i - 2
                    if 0 <= j < NT:
                        sc.wait_ge(v1, (i - 1) * nV + 7)
                        if j >= NB:
                            sc.wait_ge(t1, (i - 1) * nT + 16)
                        sc.copy(xq[:, j % NB, 0:N0], xns[:, j % NBI, 0:N0]).then_inc(s1, 1)
                    else:
                        _fill(sc, s1, 1, junk_s, jsn)
                    j = i - 2
                    if 0 <= j < NT:
                        sc.wait_ge(v1, i * nV + 1)
                        if j >= NB:
                            sc.wait_ge(t1, (i - 2) * nT + 8)
                        sc.activation(xn2s[:, j % NBI, :], xns[:, j % NBI, :], AF.Identity,
                                      bias=nm2t[:, j % 2:j % 2 + 1],
                                      scale=r2a[:, j:j + 1]).then_inc(s1, 1)
                    else:
                        _fill(sc, s1, 1, junk_s, jsn)
                    j = i - 3
                    if 0 <= j < NT:
                        sc.wait_ge(t1, i * nT + 8)
                        if j >= 3:
                            sc.wait_ge(spx[j % 3], 16 * (j // 3))
                        for c in range(KD):
                            sc.activation(x2T[:, j % 3, c, :], ps_sh[j % NB][:, 128 * c:128 * (c + 1)],
                                          AF.Identity, bias=bet_s[:, c:c + 1],
                                          scale=gam_s[:, c:c + 1]).then_inc(s1, 1)
                        sc.wait_ge(t1, i * nT + 16)
                        sc.copy(xqT[:, j % NB, :, :].rearrange("p a b -> p (a b)"),
                                bass.AP(ps_q8, 0, [[2 * D, 128], [2, D]])).then_inc(s1, 1)
                        sc.wait_ge(s1, i * nS + 12)
                        sc.dma_start(x2t_d[128 * j:128 * (j + 1), :],
                                     x2T[:, j % 3, :, :]).then_inc(spx[j % 3], 16)
                    else:
                        _fill(sc, s1, 9, junk_s, jsn)
                    j = i - 4
                    if 0 <= j < NT:
                        sq = j // TPS
                        tr = (j % TPS) * 128
                        sc.wait_ge(t1, i * nT + 25)
                        sc.copy(d_tok[:, sq, W + tr:W + tr + 128],
                                ps_db[0:16, j % NB, :]).then_inc(s1, 1)
                        sc.copy(b_tok[:, sq, W + tr:W + tr + 128],
                                ps_db[32:48, j % NB, :]).then_inc(s1, 1)
                    else:
                        _fill(sc, s1, 2, junk_s, jsn)

            @blk.gpsimd
            def _(gp):
                for i in range(NT + 4):
                    j = i - 2
                    if 0 <= j < NT:
                        gp.wait_ge(v1, (i - 1) * nV + 7)
                        if j >= NB:
                            gp.wait_ge(t1, (i - 1) * nT + 16)
                        gp.tensor_scalar(xq[:, j % NB, N0:D], xns[:, j % NBI, N0:D], FP8MAX, -FP8MAX,
                                         ALU.min, ALU.max).then_inc(g1, 1)
                    else:
                        _fill(gp, g1, 1, junk_g, jgn)
                    j = i - 2
                    if 0 <= j < NT:
                        if j >= 3:
                            gp.wait_ge(v1, (i - 1) * nV + 13)
                        gp.tensor_add(s_tmp[:, j % 3, :], xs[:, j % NBX, :],
                                      xns[:, j % NBI, :]).then_inc(g1, 1)
                    else:
                        _fill(gp, g1, 1, junk_g, jgn)
                    if i == 0:
                        gp.dma_start(wg_hi[:, :, :],
                                     bass.AP(wg_d.tensor, 0, [[D, 128], [128 * D, KD], [1, D]])).then_inc(pdh, 16)
                        for c in range(2):
                            gp.dma_start(wgf_r[:, c, :],
                                         wg_d[128 * c:128 * (c + 1), :]).then_inc(pdw[c], 16)
                    c = i - WG0
                    if 0 <= c < KD:
                        if c == 0:
                            gp.wait_ge(pdh, 16)
                        gp.wait_ge(pdw[c % 2], 16 * (c // 2 + 1))
                        gp.tensor_tensor(wg_lo[:, c, :], wgf_r[:, c % 2, :],
                                         wg_hi[:, c, :], ALU.subtract).then_inc(pL, 1)
                        if c + 2 < KD:
                            gp.wait_ge(pL, c + 1)
                            gp.dma_start(wgf_r[:, c % 2, :],
                                         wg_d[128 * (c + 2):128 * (c + 3), :]).then_inc(pdw[c % 2], 16)

            @blk.tensor
            def _(t):
                for i in range(NT + 4):
                    j = i - 3
                    if 0 <= j < NT:
                        t.wait_ge(s1, (j + 2) * nS + 4)
                        if j >= NB:
                            t.wait_ge(s1, (i - 2) * nS + 12)
                        for c in range(KD):
                            tr = t.transpose(ps_sh[j % NB][:, 128 * c:128 * (c + 1)],
                                             xn2s[:, j % NBI, 128 * c:128 * (c + 1)],
                                             ident[:, :])
                            if c == KD - 1:
                                tr.then_inc(t1, 8)
                        t.wait_ge(s1, (i - 1) * nS + 3)
                        if j >= 1:
                            t.wait_ge(s1, (i - 1) * nS + 13)
                        for c in range(KD // 2):
                            t.transpose(bass.AP(ps_q8, 2 * 128 * c, [[2 * D, 128], [2, 128]]),
                                        xq[:, j % NB, 128 * c:128 * (c + 1)],
                                        ident8[:, :])
                        t.wait_ge(g1, (i - 1) * nG + 1)
                        for c in range(KD // 2, KD):
                            tr = t.transpose(bass.AP(ps_q8, 2 * 128 * c, [[2 * D, 128], [2, 128]]),
                                             xq[:, j % NB, 128 * c:128 * (c + 1)],
                                             ident8[:, :])
                            if c == KD - 1:
                                tr.then_inc(t1, 8)
                    else:
                        _fill(t, t1, 16, None, None)
                    j = i - 4
                    if 0 <= j < NT:
                        t.wait_ge(s1, (i - 1) * nS + 12)
                        if j >= 1:
                            t.wait_ge(s1, (i - 1) * nS + 15)
                        for c in range(KD):
                            t.matmul(ps_db[:, j % NB, :], wdb[:, c, :], x2T[:, j % 3, c, :],
                                     start=(c == 0), stop=False)
                        t.matmul(ps_db[:, j % NB, :], bdb[0:1, :], ones_r[0:1, :],
                                 start=False, stop=True).then_inc(t1, 9)
                        t.wait_ge(s1, (i - 1) * nS + 13)
                        if j == 0:
                            t.wait_ge(pdq, 16 * KD)
                        if j >= 1:
                            t.wait_ge(v1, (i - 1) * nV + 13)
                        for h in range(2):
                            for c in range(0, KD, 2):
                                t.matmul(ps_ffn[:, h * N0:(h + 1) * N0], xqT[:, j % NB, c:c + 2, :],
                                         wqT[:, c:c + 2, h * N0:(h + 1) * N0],
                                         start=(c == 0), stop=False,
                                         perf_mode=mybir.MatmulPerfMode.DoubleRow)
                            mm = t.matmul(ps_ffn[:, h * N0:(h + 1) * N0], ones2b[0:2, :],
                                          fb2[0:2, h * N0:(h + 1) * N0], start=False, stop=True)
                            if h == 1:
                                mm.then_inc(t1, 10)
                    else:
                        _fill(t, t1, 19, None, None)

        p1_st.close()

        p2_st = ExitStack()
        ps_sc = [p2_st.enter_context(nc.psum_tensor(f"ps_sc{i}", [S, LB], F32)) for i in range(4)]

        def ap_tok(tens, g, col, n, step):
            return bass.AP(tens, g * WT + col, [[n_seq * WT, S], [step, n]])

        with nc.Block() as blk:
            @blk.scalar
            def _(sc):
                sc.activation(d_tok[:, :, :], d_tok[:, :, :], AF.Sigmoid).then_inc(s2s, 1)
                for i in range(NSTEP):
                    for g in range(n_seq):
                        if i == 0:
                            sc.wait_ge(t2s, 1 + g)
                        else:
                            sc.wait_ge(t2s, 2 + 4 * (i - 1) + 2 + 2 * g)
                        sc.activation(ap_tok(h_tok, g, i, LB, L),
                                      ps_sc[2 * g + i % 2][:, :], AF.Tanh).then_inc(s2s, 1)

            @blk.vector
            def _(v):
                for i in range(1, NSTEP):
                    for g in range(n_seq):
                        v.wait_ge(s2s, 1 + 2 * (i - 1) + g + 1)
                        v.tensor_mul(hdb[:, i % 2, g, :],
                                     ap_tok(h_tok, g, i - 1, LB, L),
                                     ap_tok(d_tok, g, i, LB, L)).then_inc(v2s, 1)

            @blk.tensor
            def _(t):
                t.wait_ge(s2s, 1)
                t.matmul(ps_sc[0][:, :], ident[0:S, 0:S], ap_tok(b_tok, 0, 0, LB, L),
                         start=True, stop=True).then_inc(t2s, 1)
                t.matmul(ps_sc[2][:, :], ident[0:S, 0:S], ap_tok(b_tok, 1, 0, LB, L),
                         start=True, stop=True).then_inc(t2s, 1)
                for i in range(1, NSTEP):
                    for g in range(n_seq):
                        if i >= 2:
                            t.wait_ge(s2s, 1 + 2 * (i - 2) + g + 1)
                        t.matmul(ps_sc[2 * g + i % 2][:, :], ident[0:S, 0:S],
                                 ap_tok(b_tok, g, i, LB, L),
                                 start=True, stop=False).then_inc(t2s, 1)
                        t.wait_ge(v2s, 2 * (i - 1) + g + 1)
                        t.matmul(ps_sc[2 * g + i % 2][:, :], a16[:, :], hdb[:, i % 2, g, :],
                                 start=False, stop=True).then_inc(t2s, 1)

        p2_st.close()

        nV3, nS3, nT3, nG3 = 8, 2, 54, 1
        p3_st = ExitStack()
        t3hi = p3_st.enter_context(nc.sbuf_tensor("t3hi", [128, 3, KD, 128], BF16))
        t3lo = p3_st.enter_context(nc.sbuf_tensor("t3lo", [128, 3, KD, 128], BF16))
        gsb = p3_st.enter_context(nc.sbuf_tensor("gsb", [128, NB, D], F32))
        hs3 = p3_st.enter_context(nc.sbuf_tensor("hs3", [3 * S, NBP, 128], BF16))
        hs_tl = p3_st.enter_context(nc.sbuf_tensor("hs_tl", [S, NBP, 128], BF16))
        ps_sh3 = [p3_st.enter_context(nc.psum_tensor(f"ps_sh3{i}", [128, D], F32)) for i in range(3)]
        ps_g = [p3_st.enter_context(nc.psum_tensor(f"ps_g{i}", [128, N0], F32)) for i in range(2)]

        with nc.Block() as blk:
            @blk.sync
            def _(sync):
                for k in range(NT + 5):
                    j = k - 5
                    if 0 <= j < NT:
                        sync.wait_ge(g3, (j + 4) * nG3)
                        sync.dma_start(out_d[128 * j:128 * (j + 1), :], part[:, j % NB, :]).then_inc(so3[j % NB], 16)
                    if k < NT:
                        if k < 3:
                            sync.wait_ge(spx[k], 16 * ((NT - k + 2) // 3))
                        else:
                            sync.wait_ge(v3, (k - 1) * nV3 + 2)
                        sync.dma_start(x2T[:, k % 3, :, :],
                                       x2t_d[128 * k:128 * (k + 1), :]).then_inc(sxt[k % 3], 16)
                        if k >= NBP:
                            sync.wait_ge(g3, k * nG3)
                        sync.dma_start(xns[:, k % NBP, :], out_d[128 * k:128 * (k + 1), :]).then_inc(spi[k % NBP], 16)
                        sync.wait_ge(v3, k * nV3 + 8)
                        sync.dma_start(hs3[S:2 * S, k % NBP, :], hs3[0:S, k % NBP, :]).then_inc(sdh[k % NBP], 16)
                        sync.dma_start(hs3[2 * S:3 * S, k % NBP, :], hs_tl[:, k % NBP, :]).then_inc(sdh[k % NBP], 16)
                for j in range(NB):
                    sync.wait_ge(so3[j], 16 * ((NT - j + NB - 1) // NB))

            @blk.vector
            def _(v):
                for k in range(NT + 3):
                    j = k - 2
                    if 0 <= j < NT:
                        v.wait_ge(sxt[j % 3], 16 * (j // 3 + 1))
                        if j >= 3:
                            v.wait_ge(t3, (k - 2) * nT3 + 40)
                        v.tensor_copy(t3hi[:, j % 3, :, :], x2T[:, j % 3, :, :]).then_inc(v3, 1)
                        v.wait_ge(v3, k * nV3 + 1)
                        if j >= 3:
                            v.wait_ge(t3, (k - 2) * nT3 + 48)
                        v.tensor_tensor(t3lo[:, j % 3, :, :], x2T[:, j % 3, :, :], t3hi[:, j % 3, :, :],
                                        ALU.subtract).then_inc(v3, 1)
                    else:
                        _fill(v, v3, 2, junk_v, jvn)
                    j = k - 3
                    if 0 <= j < NT:
                        v.wait_ge(t3, k * nT3 + 24)
                        if j >= NB:
                            v.wait_ge(s3, (k - 2) * nS3 + 1)
                        v.tensor_add(gsb[:, j % NB, 0:N0], ps_g[0][:, :], bgate_bc[:, 0:N0]).then_inc(v3, 1)
                        v.wait_ge(t3, k * nT3 + 48)
                        if j >= NB:
                            v.wait_ge(s3, (k - 2) * nS3 + 2)
                        v.tensor_add(gsb[:, j % NB, N0:D], ps_g[1][:, :], bgate_bc[:, N0:D]).then_inc(v3, 1)
                        v.wait_ge(t3, k * nT3 + 54)
                        v.wait_ge(s3, k * nS3 + 1)
                        if k >= 4:
                            v.wait_ge(g3, k * nG3)
                        v.tensor_mul(s_tmp[:, j % NB, 0:N0], ps_sh3[j % 3][:, 0:N0],
                                     gsb[:, j % NB, 0:N0]).then_inc(v3, 1)
                        v.wait_ge(s3, k * nS3 + 2)
                        v.tensor_mul(s_tmp[:, j % NB, N0:D], ps_sh3[j % 3][:, N0:D],
                                     gsb[:, j % NB, N0:D]).then_inc(v3, 1)
                    else:
                        _fill(v, v3, 4, junk_v, jvn)
                    if k < NT:
                        sq3 = k // TPS
                        tr3 = (k % TPS) * 128
                        if k == 0:
                            v.wait_ge(s2s, 1 + 2 * NSTEP)
                        if k >= NBP:
                            v.wait_ge(t3, (k - 1) * nT3 + 54)
                        v.tensor_copy(hs3[0:S, k % NBP, :],
                                      h_tok[:, sq3, W + tr3:W + tr3 + 128]).then_inc(v3, 1)
                        v.wait_ge(v3, k * nV3 + 7)
                        v.tensor_tensor(hs_tl[:, k % NBP, :],
                                        h_tok[:, sq3, W + tr3:W + tr3 + 128],
                                        hs3[0:S, k % NBP, :], ALU.subtract).then_inc(v3, 1)
                    else:
                        _fill(v, v3, 2, junk_v, jvn)

            @blk.scalar
            def _(sc):
                for k in range(NT + 3):
                    j = k - 3
                    if 0 <= j < NT:
                        sc.wait_ge(v3, k * nV3 + 3)
                        sc.activation(gsb[:, j % NB, 0:N0], gsb[:, j % NB, 0:N0], AF.Sigmoid).then_inc(s3, 1)
                        sc.wait_ge(v3, k * nV3 + 4)
                        sc.activation(gsb[:, j % NB, N0:D], gsb[:, j % NB, N0:D], AF.Sigmoid).then_inc(s3, 1)
                    else:
                        _fill(sc, s3, 2, junk_s, jsn)

            @blk.gpsimd
            def _(gp):
                for k in range(NT + 3):
                    j = k - 3
                    if 0 <= j < NT:
                        gp.wait_ge(v3, k * nV3 + 6)
                        gp.wait_ge(spi[j % NBP], 16 * (j // NBP + 1))
                        if j >= NB:
                            gp.wait_ge(so3[j % NB], 16 * ((j - NB) // NB + 1))
                        gp.tensor_add(part[:, j % NB, :], s_tmp[:, j % NB, :],
                                      xns[:, j % NBP, :]).then_inc(g3, 1)
                    else:
                        _fill(gp, g3, 1, junk_g, jgn)

            @blk.tensor
            def _(t):
                for k in range(NT + 3):
                    j = k - 3
                    if 0 <= j < NT:
                        t.wait_ge(v3, (k - 1) * nV3 + 2)
                        if j == 0:
                            t.wait_ge(pL, KD)
                        for h in range(2):
                            if j >= 1:
                                t.wait_ge(v3, (k - 1) * nV3 + 3 + h)
                            for c in range(KD):
                                t.matmul(ps_g[h][:, :], t3hi[:, j % 3, c, :],
                                         wg_hi[:, c, h * N0:(h + 1) * N0],
                                         start=(c == 0), stop=False)
                                mm = t.matmul(ps_g[h][:, :], t3hi[:, j % 3, c, :],
                                              wg_lo[:, c, h * N0:(h + 1) * N0],
                                              start=False, stop=False)
                                if h == 1 and c == KD - 1:
                                    mm.then_inc(t3, 16)
                            for c in range(KD):
                                mm = t.matmul(ps_g[h][:, :], t3lo[:, j % 3, c, :],
                                              wg_hi[:, c, h * N0:(h + 1) * N0],
                                              start=False, stop=(c == KD - 1))
                                if c == KD - 1:
                                    mm.then_inc(t3, 24 if h == 0 else 8)
                        t.wait_ge(v3, (k - 3) * nV3 + 8)
                        t.wait_ge(sdh[j % NBP], 32 * (j // NBP + 1))
                        if j == 0:
                            t.wait_ge(pcm, 48)
                        for h in range(2):
                            t.matmul(ps_sh3[j % 3][:, h * N0:(h + 1) * N0], hs3[:, j % NBP, :],
                                     cm3[:, h * N0:(h + 1) * N0], start=True, stop=True)
                        t.ldweights(ones1b[0:1, 0:1]).then_inc(t3, 6)
                    else:
                        _fill(t, t3, 54, None, None)
        p3_st.close()

    return nc


def kernel(L0_attn_x, L0_attn_ln_gamma, L0_attn_ln_beta, L0_attn_A, L0_attn_C,
           L0_attn_w_delta, L0_attn_b_delta, L0_attn_w_in, L0_attn_b_in,
           L0_attn_w_gate, L0_attn_b_gate, L0_ffn_weight, L0_ffn_bias):
    B, T, _ = L0_attn_x.shape
    n_cores = 8
    spc = B // n_cores
    x = np.ascontiguousarray(np.asarray(L0_attn_x, dtype=np.float32))
    common = {
        "w_gate": np.ascontiguousarray(np.asarray(L0_attn_w_gate, np.float32)),
        "w_delta": np.ascontiguousarray(np.asarray(L0_attn_w_delta, np.float32)),
        "w_in": np.ascontiguousarray(np.asarray(L0_attn_w_in, np.float32)),
        "b_gate": np.asarray(L0_attn_b_gate, np.float32).reshape(1, -1),
        "b_delta": np.asarray(L0_attn_b_delta, np.float32).reshape(1, -1),
        "b_in": np.asarray(L0_attn_b_in, np.float32).reshape(1, -1),
        "ln_gamma": np.asarray(L0_attn_ln_gamma, np.float32).reshape(1, -1),
        "ln_beta": np.asarray(L0_attn_ln_beta, np.float32).reshape(1, -1),
        "a_mat": np.ascontiguousarray(np.asarray(L0_attn_A, np.float32)),
        "c_mat": np.ascontiguousarray(np.asarray(L0_attn_C, np.float32)),
        "ffn_wT": np.ascontiguousarray(np.asarray(L0_ffn_weight, np.float32).T),
        "ffn_b": np.asarray(L0_ffn_bias, np.float32).reshape(1, -1),
    }
    in_maps = []
    for c in range(n_cores):
        m = dict(common)
        m["x"] = np.ascontiguousarray(x[c * spc:(c + 1) * spc].reshape(spc * T, -1))
        in_maps.append(m)
    nc = build_nc(n_seq=spc, T=T)
    res = run_bass_kernel_spmd(nc, in_maps, core_ids=list(range(n_cores)))
    outs = [np.asarray(r["out"]).reshape(spc, T, -1) for r in res.results]
    return np.concatenate(outs, axis=0)
